# version 26
# baseline (speedup 1.0000x reference)
import numpy as np
import ml_dtypes
import concourse.bass as bass
import concourse.mybir as mybir
from concourse.bass_utils import run_bass_kernel_spmd
from contextlib import ExitStack

F32 = mybir.dt.float32
BF16 = mybir.dt.bfloat16
I32 = mybir.dt.int32
AF = mybir.ActivationFunctionType
ALU = mybir.AluOpType
AX = mybir.AxisListType

NCORES = 8
BPC = 2
SEQ = 4096
H = 1024
G = 8
GW = 512
NSUB = 4
QSC = float(1.0 / np.sqrt(512.0))
EPS = 1e-5

ENGINES = ("sync", "gpsimd", "vector", "scalar", "tensor")


class Sched:
    def __init__(self, sems):
        self.sems = sems
        self.cnt = {k: 0 for k in sems}
        self.ops = {e: [] for e in ENGINES}
        self.waited = {e: {k: 0 for k in sems} for e in ENGINES}

    def wait(self, eng, name, val):
        if val is None or val <= 0:
            return
        if self.waited[eng][name] >= val:
            return
        self.waited[eng][name] = val
        sem = self.sems[name]
        self.ops[eng].append(lambda e, sem=sem, val=val: e.wait_ge(sem, val))

    def op(self, eng, build, inc=None, waits=()):
        for w in waits:
            if w is not None:
                self.wait(eng, w[0], w[1])
        incs = []
        ret = {}
        if inc is not None:
            if isinstance(inc, tuple):
                inc = [inc]
            for (name, k) in inc:
                self.cnt[name] += k
                incs.append((self.sems[name], k))
                ret[name] = self.cnt[name]
        def run(e, build=build, incs=tuple(incs)):
            inst = build(e)
            for (sem, k) in incs:
                inst = inst.then_inc(sem, k)
        self.ops[eng].append(run)
        return ret

    def flush(self, block):
        for eng in ENGINES:
            ops = self.ops[eng]
            def body(e, ops=ops):
                for f in ops:
                    f(e)
            getattr(block, eng)(body)
        self.ops = {e: [] for e in ENGINES}


def build_nc(dump=False):
    nc = bass.Bass("TRN2", target_bir_lowering=False, debug=False)

    d_x = nc.dram_tensor("x", [BPC * SEQ, H], F32, kind="ExternalInput").ap()
    d_wqT = nc.dram_tensor("wqT", [H, H], F32, kind="ExternalInput").ap()
    d_wkT = nc.dram_tensor("wkT", [H, H], F32, kind="ExternalInput").ap()
    d_wvT = nc.dram_tensor("wvT", [H, H], F32, kind="ExternalInput").ap()
    d_woT = nc.dram_tensor("woT", [H, H], F32, kind="ExternalInput").ap()
    d_wf = nc.dram_tensor("wf", [H, H], F32, kind="ExternalInput").ap()
    d_bql = nc.dram_tensor("bql", [128, 8], F32, kind="ExternalInput").ap()
    d_bkl = nc.dram_tensor("bkl", [128, 8], F32, kind="ExternalInput").ap()
    d_bvr = nc.dram_tensor("bvr", [1, H], F32, kind="ExternalInput").ap()
    d_bor = nc.dram_tensor("bor", [1, H], F32, kind="ExternalInput").ap()
    d_bfr = nc.dram_tensor("bfr", [1, H], F32, kind="ExternalInput").ap()
    d_out = nc.dram_tensor("out", [BPC * SEQ, H], F32, kind="ExternalOutput").ap()

    scr = "ExternalOutput" if dump else "Internal"
    d_qt = nc.dram_tensor("qt_d", [BPC, G, 8, 128, GW], BF16, kind=scr).ap()
    d_kt = nc.dram_tensor("kt_d", [BPC, G, 8, 128, GW], BF16, kind=scr).ap()
    d_v = nc.dram_tensor("v_d", [BPC, G, 4, 128, H], BF16, kind=scr).ap()
    d_wd = nc.dram_tensor("wd_d", [H, H], BF16, kind=scr).ap()
    d_ln1 = nc.dram_tensor("ln1_d", [2, GW, H], BF16, kind="Internal").ap()
    d_l = nc.dram_tensor("l_d", [2, GW], F32, kind="Internal").ap()

    root = ExitStack()
    with root:
        semnames = ["s_lx", "s_lw", "s_ms", "s_l1w", "s_l1r", "s_qkw", "s_kvr",
                    "s_ldr", "s_xtr", "s_ow", "s_pe", "s_sc", "s_dv"]
        sems = {n: root.enter_context(nc.semaphore(n)) for n in semnames}
        S = Sched(sems)

        def sb(st, name, shape, dt):
            return st.enter_context(nc.sbuf_tensor(name, shape, dt))

        bql = sb(root, "bql_s", [128, 8], F32)
        bkl = sb(root, "bkl_s", [128, 8], F32)
        rowf = sb(root, "rowf", [1, 1544], F32)
        rowbf = sb(root, "rowbf", [1, 1024], BF16)
        biasrow = sb(root, "biasrow", [1, 3072], BF16)
        kmax = sb(root, "kmax", [1, 3], F32)
        lt = sb(root, "l_t", [128, 4], F32)
        invl = sb(root, "invl", [128, 4], F32)
        ones_c = sb(root, "ones_c", [128, 1], BF16)
        ones_r = sb(root, "ones_r", [1, 128], BF16)
        nones_r = sb(root, "nones_r", [1, 128], BF16)
        st_ = sb(root, "stats", [128, 128], F32)
        sqsc = sb(root, "sqsc", [128, 1024], BF16)
        sqb = sb(root, "sqb", [128, 1024], BF16)
        qv = sb(root, "qv", [128, 1024], F32)
        qa = sb(root, "qa", [128, 1024], F32)
        qg2 = sb(root, "qg2", [128, 1024], BF16)
        qam = sb(root, "qam", [128, 32], F32)
        qsb = sb(root, "qsb", [128, 32], F32)
        qis = sb(root, "qis", [128, 32], F32)

        psA = root.enter_context(nc.psum_tensor("psA", [128, 2048], F32))
        psB = root.enter_context(nc.psum_tensor("psB", [128, 2048], F32))
        bankA = [psA[:, i * 512:(i + 1) * 512] for i in range(4)]
        bankB = [psB[:, i * 512:(i + 1) * 512] for i in range(4)]
        row_ps = psB[0:1, 1536:2048]
        bankA_ev = [[] for _ in range(4)]
        bankB_ev = [[] for _ in range(4)]

        def vop(build, waits=()):
            ev = S.op("vector", build, inc=("s_dv", 1), waits=waits)
            S.wait("vector", "s_dv", ev["s_dv"])
            return ev

        def slot(idx):
            base = idx * 8
            return {k: st_[:, base + i:base + i + 1] for i, k in enumerate(
                ["sum", "ssq", "mean", "msq", "var", "rstd", "nb", "scl"])}

        def ln_stats4(srcs, n, slot_base, waits_dv, waits_sc, extra_scales=None,
                      dve_sq=False):
            ns = len(srcs)
            C = [slot(slot_base + i) for i in range(ns)]
            ev_red = [None] * ns
            for i in range(ns):
                ev_red[i] = vop(lambda e, i=i: nc.vector.tensor_reduce(
                    C[i]["sum"], srcs[i], AX.X, ALU.add),
                    waits=waits_dv[i] if waits_dv else ())
            ev_sq = [None] * ns
            if dve_sq:
                for i in range(ns):
                    vop(lambda e, i=i: nc.vector.tensor_tensor(
                        qa[:, 0:n], srcs[i], srcs[i], ALU.mult))
                    ev_sq[i] = vop(lambda e, i=i: nc.vector.tensor_reduce(
                        C[i]["ssq"], qa[:, 0:n], AX.X, ALU.add))
            else:
                for i in range(ns):
                    ev_sq[i] = S.op("scalar", lambda e, i=i: e.activation(
                        sqsc[:, 0:n], srcs[i], AF.Square, accum_out=C[i]["ssq"]),
                        inc=("s_sc", 1),
                        waits=(list(waits_sc[i]) if waits_sc else []) +
                              [("s_dv", ev_red[i]["s_dv"])])
            for i in range(ns):
                wsq = [] if dve_sq else [("s_sc", ev_sq[i]["s_sc"])]
                xs = extra_scales[i] if extra_scales else None
                if xs is not None:
                    vop(lambda e, i=i, xs=xs: nc.vector.tensor_scalar(
                        C[i]["mean"], C[i]["sum"], 1.0 / n, xs, ALU.mult, ALU.mult),
                        waits=wsq)
                    vop(lambda e, i=i, xs=xs: nc.vector.tensor_scalar(
                        C[i]["ssq"], C[i]["ssq"], xs, xs, ALU.mult, ALU.mult))
                else:
                    vop(lambda e, i=i: nc.vector.tensor_single_scalar(
                        C[i]["mean"], C[i]["sum"], 1.0 / n, ALU.mult),
                        waits=wsq)
            for i in range(ns):
                vop(lambda e, i=i: nc.vector.tensor_tensor(
                    C[i]["msq"], C[i]["mean"], C[i]["mean"], ALU.mult))
            for i in range(ns):
                vop(lambda e, i=i: nc.vector.scalar_tensor_tensor(
                    C[i]["var"], C[i]["ssq"], 1.0 / n, C[i]["msq"],
                    ALU.mult, ALU.subtract))
            ev_var = [None] * ns
            for i in range(ns):
                ev_var[i] = vop(lambda e, i=i: nc.vector.tensor_single_scalar(
                    C[i]["var"], C[i]["var"], EPS, ALU.add))
            for i in range(ns):
                S.op("scalar", lambda e, i=i: e.activation(
                    C[i]["msq"], C[i]["var"], AF.Ln),
                    waits=[("s_dv", ev_var[i]["s_dv"])])
            ev_r = [None] * ns
            for i in range(ns):
                ev_r[i] = S.op("scalar", lambda e, i=i: e.activation(
                    C[i]["rstd"], C[i]["msq"], AF.Exp, scale=-0.5),
                    inc=("s_sc", 1))
            out = []
            for i in range(ns):
                xs = extra_scales[i] if extra_scales else None
                if xs is not None:
                    vop(lambda e, i=i, xs=xs: nc.vector.tensor_tensor(
                        C[i]["scl"], C[i]["rstd"], xs, ALU.mult),
                        waits=[("s_sc", ev_r[i]["s_sc"])])
                ev = vop(lambda e, i=i: nc.vector.tensor_scalar(
                    C[i]["nb"], C[i]["mean"], C[i]["rstd"], -1.0,
                    ALU.mult, ALU.mult), waits=[("s_sc", ev_r[i]["s_sc"])])
                out.append((C[i]["scl"] if xs is not None else C[i]["rstd"],
                            C[i]["nb"], ev["s_dv"]))
            return out

        def emit_quant(src_ap, out_ap, waits_dv=()):
            src3 = src_ap.rearrange("p (b i) -> p b i", i=32)
            vop(lambda e: nc.vector.tensor_reduce(
                qam[:], src3, AX.X, ALU.max, apply_absolute_value=True),
                waits=waits_dv)
            vop(lambda e: nc.vector.tensor_scalar(
                qsb[:], qam[:], 1.0 / 7.5, 1e-30, ALU.mult, ALU.max))
            vop(lambda e: nc.vector.reciprocal(qis[:], qsb[:]))
            isb = qis[:].unsqueeze(2).broadcast_to([128, 32, 32])
            ssb = qsb[:].unsqueeze(2).broadcast_to([128, 32, 32])
            ev_v = vop(lambda e: nc.vector.tensor_tensor(
                qv[:].rearrange("p (b i) -> p b i", i=32), src3, isb, ALU.mult))
            ev_a = S.op("scalar", lambda e: e.activation(qa[:], qv[:], AF.Abs),
                        inc=("s_sc", 1), waits=[("s_dv", ev_v["s_dv"])])
            vop(lambda e: nc.vector.tensor_single_scalar(
                qg2[:], qa[:], 2.0, ALU.is_ge), waits=[("s_sc", ev_a["s_sc"])])
            vop(lambda e: nc.vector.scalar_tensor_tensor(
                qg2[:], qa[:], 4.0, qg2[:], ALU.is_ge, ALU.add))
            vop(lambda e: nc.vector.scalar_tensor_tensor(
                qa[:], qg2[:], -5.0, qg2[:], ALU.add, ALU.mult))
            vop(lambda e: nc.vector.tensor_single_scalar(
                qa[:], qa[:], 8.0, ALU.add))
            vop(lambda e: nc.vector.tensor_tensor(qv[:], qv[:], qa[:], ALU.mult))
            vop(lambda e: nc.vector.tensor_single_scalar(
                qv[:], qv[:], 12582912.0, ALU.add))
            vop(lambda e: nc.vector.tensor_single_scalar(
                qv[:], qv[:], -12582912.0, ALU.add))
            vop(lambda e: nc.vector.scalar_tensor_tensor(
                qa[:], qg2[:], 1.0, qg2[:], ALU.add, ALU.mult))
            vop(lambda e: nc.vector.tensor_scalar(
                qa[:], qa[:], 0.0625, 0.125, ALU.mult, ALU.add))
            vop(lambda e: nc.vector.tensor_tensor(qv[:], qv[:], qa[:], ALU.mult))
            ev = vop(lambda e: nc.vector.tensor_tensor(
                out_ap.rearrange("p (b i) -> p b i", i=32),
                qv[:].rearrange("p (b i) -> p b i", i=32), ssb, ALU.mult))
            return ev["s_dv"]

        def emit_quant2(xm, rstd, nb3, nrstd, nnb, out_ap, waits_dv=()):
            xm3 = xm.rearrange("p (b i) -> p b i", i=32)
            ev1 = S.op("vector", lambda e: nc.vector.tensor_reduce(
                qam[:], xm3, AX.X, ALU.max), inc=("s_dv", 1), waits=waits_dv)
            ev2 = S.op("vector", lambda e: nc.vector.tensor_reduce(
                qsb[:], xm3, AX.X, ALU.min), inc=("s_dv", 1))
            S.wait("vector", "s_dv", ev2["s_dv"])
            vop(lambda e: nc.vector.tensor_scalar(
                qam[:], qam[:], rstd, nb3, ALU.mult, ALU.add))
            vop(lambda e: nc.vector.tensor_scalar(
                qsb[:], qsb[:], nrstd, nnb, ALU.mult, ALU.add))
            vop(lambda e: nc.vector.tensor_tensor(qam[:], qam[:], qsb[:], ALU.max))
            vop(lambda e: nc.vector.tensor_scalar(
                qsb[:], qam[:], 1.0 / 7.5, 1e-30, ALU.mult, ALU.max))
            vop(lambda e: nc.vector.reciprocal(qis[:], qsb[:]))
            vop(lambda e: nc.vector.tensor_scalar(
                qam[:], qis[:], rstd, None, ALU.mult))
            ev3 = vop(lambda e: nc.vector.tensor_scalar(
                qis[:], qis[:], nb3, None, ALU.mult))
            rsb = qam[:].unsqueeze(2).broadcast_to([128, 32, 32])
            mbb = qis[:].unsqueeze(2).broadcast_to([128, 32, 32])
            ssb = qsb[:].unsqueeze(2).broadcast_to([128, 32, 32])
            S.op("vector", lambda e: nc.vector.tensor_tensor(
                qv[:].rearrange("p (b i) -> p b i", i=32), xm3, rsb, ALU.mult))
            S.op("vector", lambda e: nc.vector.tensor_tensor(
                qv[:].rearrange("p (b i) -> p b i", i=32),
                qv[:].rearrange("p (b i) -> p b i", i=32), mbb, ALU.add))
            S.op("vector", lambda e: nc.vector.tensor_tensor(
                qa[:], qv[:], qv[:], ALU.mult))
            S.op("vector", lambda e: nc.vector.tensor_single_scalar(
                qg2[:], qa[:], 4.0, ALU.is_ge))
            S.op("vector", lambda e: nc.vector.scalar_tensor_tensor(
                qg2[:], qa[:], 16.0, qg2[:], ALU.is_ge, ALU.add))
            S.op("vector", lambda e: nc.vector.scalar_tensor_tensor(
                qa[:], qg2[:], -5.0, qg2[:], ALU.add, ALU.mult))
            S.op("vector", lambda e: nc.vector.tensor_single_scalar(
                qa[:], qa[:], 8.0, ALU.add))
            S.op("vector", lambda e: nc.vector.tensor_tensor(
                qv[:], qv[:], qa[:], ALU.mult))
            S.op("vector", lambda e: nc.vector.tensor_single_scalar(
                qv[:], qv[:], 12582912.0, ALU.add))
            S.op("vector", lambda e: nc.vector.tensor_single_scalar(
                qv[:], qv[:], -12582912.0, ALU.add))
            S.op("vector", lambda e: nc.vector.scalar_tensor_tensor(
                qa[:], qg2[:], 1.0, qg2[:], ALU.add, ALU.mult))
            S.op("vector", lambda e: nc.vector.tensor_scalar(
                qa[:], qa[:], 0.0625, 0.125, ALU.mult, ALU.add))
            S.op("vector", lambda e: nc.vector.tensor_tensor(
                qv[:], qv[:], qa[:], ALU.mult))
            ev = S.op("vector", lambda e: nc.vector.tensor_tensor(
                out_ap.rearrange("p (b i) -> p b i", i=32),
                qv[:].rearrange("p (b i) -> p b i", i=32), ssb, ALU.mult),
                inc=("s_dv", 1))
            S.wait("vector", "s_dv", ev["s_dv"])
            return ev["s_dv"]

        with ExitStack() as stA:
            stage = sb(stA, "stage", [128, 2048], F32)
            wq_sb = sb(stA, "wq_sb", [128, 8192], BF16)
            wk_sb = sb(stA, "wk_sb", [128, 8192], BF16)
            wv_sb = sb(stA, "wv_sb", [128, 8192], BF16)
            x_sb = sb(stA, "x_sbA", [128, 8192], F32)
            ln1_sb = sb(stA, "ln1_sb", [128, 8192], BF16)
            ln1T_sb = sb(stA, "ln1T_sb", [128, 8192], BF16)
            qt_st = sb(stA, "qt_st", [128, 4096], BF16)
            kt_st = sb(stA, "kt_st", [128, 4096], BF16)
            v_st = sb(stA, "v_st", [128, 4096], BF16)
            wd_st = sb(stA, "wd_st", [128, 2048], BF16)
            rowstg = sb(stA, "rowstg", [1, 1024], F32)
            sqkn = sb(stA, "sqkn", [128, 4096], BF16)

            S.op("gpsimd", lambda e: e.dma_start(out=bql[:], in_=d_bql[:]),
                 inc=("s_ms", 16))
            S.op("gpsimd", lambda e: e.dma_start(out=bkl[:], in_=d_bkl[:]),
                 inc=("s_ms", 16))
            ev_ms = None
            for i, dsrc in enumerate([d_bvr, d_bor, d_bfr]):
                ev_ms = S.op("gpsimd", lambda e, dsrc=dsrc: e.dma_start(
                    out=rowstg[:], in_=dsrc[:]), inc=("s_ms", 16),
                    waits=[("s_dv", ev_cast["s_dv"])] if i else [])
                ev_cast = vop(lambda e, i=i: nc.vector.tensor_copy(
                    biasrow[0:1, i * 1024:(i + 1) * 1024], rowstg[:]),
                    waits=[("s_ms", ev_ms["s_ms"])])
            bias_bf_ev = ev_cast["s_dv"]
            bvrow = biasrow[0:1, 0:1024]
            borow = biasrow[0:1, 1024:2048]
            bfrow = biasrow[0:1, 2048:3072]
            vop(lambda e: nc.vector.memset(ones_c[:], 1.0))
            vop(lambda e: nc.vector.memset(ones_r[:], 1.0))
            vop(lambda e: nc.vector.memset(nones_r[:], -1.0))
            vop(lambda e: nc.vector.memset(kmax[:], 0.0))
            ev_bqs = vop(lambda e: nc.vector.tensor_single_scalar(
                bql[:], bql[:], QSC, ALU.mult), waits=[("s_ms", 32)])

            stage_ev = [None, None]
            cast_done = None
            for wi, (dsrc, dst) in enumerate([(d_wqT, wq_sb), (d_wkT, wk_sb),
                                              (d_wvT, wv_sb)]):
                for c in range(8):
                    par = (wi * 8 + c) % 2
                    w = [("s_dv", stage_ev[par])] if stage_ev[par] else []
                    ev_l = S.op("sync", lambda e, dsrc=dsrc, c=c, par=par:
                                e.dma_start(
                                    out=stage[:, par * 1024:(par + 1) * 1024],
                                    in_=dsrc[c * 128:(c + 1) * 128, :]),
                                inc=("s_lw", 16), waits=w)
                    ev_c = vop(lambda e, dst=dst, c=c, par=par:
                               nc.vector.tensor_copy(
                                   dst[:, c * 1024:(c + 1) * 1024],
                                   stage[:, par * 1024:(par + 1) * 1024]),
                               waits=[("s_lw", ev_l["s_lw"])])
                    stage_ev[par] = ev_c["s_dv"]
                    cast_done = ev_c["s_dv"]

            wd_wr_ev = [None, None]
            for c in range(8):
                par = c % 2
                w = [("s_dv", stage_ev[par])] if stage_ev[par] else []
                ev_l = S.op("sync", lambda e, c=c, par=par: e.dma_start(
                    out=stage[:, par * 1024:(par + 1) * 1024],
                    in_=d_wf[c * 128:(c + 1) * 128, :]),
                    inc=("s_lw", 16), waits=w)
                qw = [("s_lw", ev_l["s_lw"])]
                if wd_wr_ev[par]:
                    qw.append(("s_qkw", wd_wr_ev[par]))
                ev_q = emit_quant(stage[:, par * 1024:(par + 1) * 1024],
                                  wd_st[:, par * 1024:(par + 1) * 1024],
                                  waits_dv=qw)
                stage_ev[par] = ev_q
                ev_w = S.op("gpsimd", lambda e, c=c, par=par: e.dma_start(
                    out=d_wd[c * 128:(c + 1) * 128, :],
                    in_=wd_st[:, par * 1024:(par + 1) * 1024]),
                    inc=("s_qkw", 16), waits=[("s_dv", ev_q)])
                wd_wr_ev[par] = ev_w["s_qkw"]

            a_apply_ev = [None, None]
            a_red_ev = [None, None]
            a_l1w_ev = [None, None]
            a_l1r_ev = [None, None]
            a_vmm_ev = [None, None]
            a_qkw_ev = None
            sqb_pe_ev = [None]

            l1r_of = {}

            def emit_ln1(gi):
                b2, g2 = divmod(gi, G)
                par = gi % 2
                r0 = b2 * SEQ + g2 * GW
                w = []
                if a_apply_ev[par]:
                    w += [("s_sc", a_apply_ev[par]), ("s_dv", a_red_ev[par])]
                ev_x = S.op("gpsimd", lambda e, r0=r0, par=par: e.dma_start(
                    out=x_sb[:, par * 4096:(par + 1) * 4096]
                        .rearrange("p (s d) -> p s d", d=1024),
                    in_=d_x[r0:r0 + GW, :].rearrange("(s p) d -> p s d", p=128)),
                    inc=("s_lx", 16), waits=w)
                srcs = [x_sb[:, par * 4096 + s * 1024:
                             par * 4096 + (s + 1) * 1024] for s in range(4)]
                stats = ln_stats4(
                    srcs, 1024, 12,
                    [[("s_lx", ev_x["s_lx"])]] * 4,
                    [[("s_lx", ev_x["s_lx"])]] * 4)
                a_red_ev[par] = S.cnt["s_dv"]
                wsc = [("s_l1w", a_l1w_ev[par])] if a_l1w_ev[par] else []
                last_apply = None
                for s in range(NSUB):
                    scl, nb, ev_st = stats[s]
                    ev_ap = S.op("scalar", lambda e, s=s, par=par, scl=scl,
                                 nb=nb, src_=srcs[s]: e.activation(
                                     ln1_sb[:, par * 4096 + s * 1024:
                                            par * 4096 + (s + 1) * 1024],
                                     src_, AF.Identity, scale=scl, bias=nb),
                                 inc=("s_sc", 1),
                                 waits=[("s_dv", ev_st)] + wsc)
                    wsc = []
                    last_apply = ev_ap["s_sc"]
                a_apply_ev[par] = last_apply
                w = [("s_sc", last_apply)]
                if a_l1r_ev[par]:
                    w.append(("s_l1r", a_l1r_ev[par]))
                ev_w = S.op("gpsimd", lambda e, par=par: e.dma_start(
                    out=d_ln1[par].rearrange("(s p) d -> p s d", p=128),
                    in_=ln1_sb[:, par * 4096:(par + 1) * 4096]
                        .rearrange("p (s d) -> p s d", d=1024)),
                    inc=("s_l1w", 16), waits=w)
                a_l1w_ev[par] = ev_w["s_l1w"]
                w = [("s_l1w", ev_w["s_l1w"])]
                if a_vmm_ev[par]:
                    w.append(("s_pe", a_vmm_ev[par]))
                ev_r = None
                for c in range(8):
                    ev_r = S.op("sync", lambda e, c=c, par=par: e.dma_start(
                        out=ln1T_sb[:, par * 4096 + c * 512:
                                    par * 4096 + (c + 1) * 512],
                        in_=d_ln1[par][:, c * 128:(c + 1) * 128],
                        transpose=True),
                        inc=("s_l1r", 16), waits=w)
                    w = []
                a_l1r_ev[par] = ev_r["s_l1r"]
                l1r_of[gi] = ev_r["s_l1r"]

            emit_ln1(0)
            for gi in range(BPC * G):
                b, g = divmod(gi, G)
                if True:
                    par = gi % 2
                    if gi + 1 < BPC * G:
                        emit_ln1(gi + 1)
                    l1r_done = l1r_of[gi]

                    pe_w = [("s_l1r", l1r_done), ("s_dv", cast_done),
                            ("s_dv", ev_bqs["s_dv"]), ("s_dv", bias_bf_ev),
                            ("s_ms", 32)]
                    for which, (wsb, bias_ap, sc_const) in enumerate(
                            [(wq_sb, bql, QSC), (wk_sb, bkl, 1.0)]):
                        for m in range(8):
                            bk_i = m % 4
                            waits = list(pe_w) + bankA_ev[bk_i]
                            for c in range(8):
                                ev_mm = S.op("tensor", lambda e, wsb=wsb, c=c,
                                             m=m, par=par, bk_i=bk_i:
                                             nc.tensor.matmul(
                                                 bankA[bk_i],
                                                 wsb[:, c * 1024 + m * 128:
                                                     c * 1024 + (m + 1) * 128],
                                                 ln1T_sb[:, par * 4096 + c * 512:
                                                         par * 4096 + (c + 1) * 512],
                                                 start=(c == 0), stop=(c == 7)),
                                             inc=("s_pe", 1) if c == 7 else None,
                                             waits=waits if c == 0 else ())
                                waits = ()
                            dst = qt_st if which == 0 else kt_st
                            wsc = [("s_pe", ev_mm["s_pe"])]
                            if a_qkw_ev:
                                wsc.append(("s_qkw", a_qkw_ev))
                            ev_ev = S.op("scalar", lambda e, dst=dst, m=m,
                                         bias_ap=bias_ap, sc_const=sc_const,
                                         bk_i=bk_i: e.activation(
                                             dst[:, m * 512:(m + 1) * 512],
                                             bankA[bk_i], AF.Identity,
                                             scale=sc_const,
                                             bias=bias_ap[:, m:m + 1]),
                                         inc=("s_sc", 1), waits=wsc)
                            bankA_ev[bk_i] = [("s_sc", ev_ev["s_sc"])]
                            if which == 1:
                                wsq = ([("s_pe", sqb_pe_ev[0])]
                                       if (m == 0 and sqb_pe_ev[0]) else [])
                                ev_sqm = S.op("scalar", lambda e, m=m:
                                              e.activation(
                                                  sqkn[:, m * 512:(m + 1) * 512],
                                                  kt_st[:, m * 512:(m + 1) * 512],
                                                  AF.Square),
                                              inc=("s_sc", 1), waits=wsq)
                                kn_sq_ev = ev_sqm["s_sc"]
                    vgrp = 0
                    for t4 in range(4):
                        for half in range(2):
                            bk_i = vgrp % 3
                            vgrp += 1
                            waits = list(pe_w) + bankB_ev[bk_i]
                            for c in range(8):
                                S.op("tensor", lambda e, c=c, t4=t4, half=half,
                                     par=par, bk_i=bk_i: nc.tensor.matmul(
                                         bankB[bk_i],
                                         ln1T_sb[:, par * 4096 + c * 512 +
                                                 t4 * 128:
                                                 par * 4096 + c * 512 +
                                                 (t4 + 1) * 128],
                                         wv_sb[:, c * 1024 + half * 512:
                                               c * 1024 + (half + 1) * 512],
                                         start=(c == 0), stop=False),
                                     waits=waits if c == 0 else ())
                                waits = ()
                            ev_mm = S.op("tensor", lambda e, half=half, bk_i=bk_i:
                                         nc.tensor.matmul(
                                             bankB[bk_i], ones_r[:],
                                             bvrow[0:1, half * 512:(half + 1) * 512],
                                             start=False, stop=True),
                                         inc=("s_pe", 1))
                            wdv = [("s_pe", ev_mm["s_pe"])]
                            if a_qkw_ev:
                                wdv.append(("s_qkw", a_qkw_ev))
                            ev_ev = vop(lambda e, t4=t4, half=half, bk_i=bk_i:
                                        nc.vector.tensor_copy(
                                            v_st[:, t4 * 1024 + half * 512:
                                                 t4 * 1024 + (half + 1) * 512],
                                            bankB[bk_i]), waits=wdv)
                            bankB_ev[bk_i] = [("s_dv", ev_ev["s_dv"])]
                    for m in range(8):
                        waits = [("s_sc", kn_sq_ev)] if m == 0 else []
                        if m == 0:
                            waits += bankB_ev[3]
                        ev_mm = S.op("tensor", lambda e, m=m: nc.tensor.matmul(
                            row_ps, ones_c[:],
                            sqkn[:, m * 512:(m + 1) * 512],
                            start=(m == 0), stop=(m == 7)),
                            inc=("s_pe", 1) if m == 7 else None, waits=waits)
                    sqb_pe_ev[0] = ev_mm["s_pe"]
                    ev_kn = S.op("scalar", lambda e: e.activation(
                        rowf[0:1, 1024:1536], row_ps, AF.Copy),
                        inc=("s_sc", 1), waits=[("s_pe", sqb_pe_ev[0])])
                    bankB_ev[3] = [("s_sc", ev_kn["s_sc"])]
                    ev_km = vop(lambda e: nc.vector.tensor_reduce(
                        rowf[0:1, 1536:1537], rowf[0:1, 1024:1536], AX.X, ALU.max),
                        waits=[("s_sc", ev_kn["s_sc"])])
                    vop(lambda e, b=b: nc.vector.tensor_tensor(
                        kmax[0:1, b:b + 1], kmax[0:1, b:b + 1],
                        rowf[0:1, 1536:1537], ALU.max))

                    wq_ = [("s_sc", S.cnt["s_sc"]), ("s_dv", ev_ev["s_dv"])]
                    S.op("sync", lambda e, b=b, g=g: e.dma_start(
                        out=d_qt[b, g].rearrange("c p f -> p c f"),
                        in_=qt_st[:].rearrange("p (c f) -> p c f", f=512)),
                        inc=("s_qkw", 16), waits=wq_)
                    S.op("sync", lambda e, b=b, g=g: e.dma_start(
                        out=d_kt[b, g].rearrange("c p f -> p c f"),
                        in_=kt_st[:].rearrange("p (c f) -> p c f", f=512)),
                        inc=("s_qkw", 16))
                    ev3 = S.op("sync", lambda e, b=b, g=g: e.dma_start(
                        out=d_v[b, g].rearrange("t p d -> p t d"),
                        in_=v_st[:].rearrange("p (t d) -> p t d", d=1024)),
                        inc=("s_qkw", 16))
                    a_qkw_ev = ev3["s_qkw"]
                    a_vmm_ev[par] = ev_mm["s_pe"]

            a_finals = dict(S.cnt)
            with nc.Block() as blockA:
                S.flush(blockA)

        with ExitStack() as stB:
            wo_sb = sb(stB, "wo_sb", [128, 8192], BF16)
            wd_sb = sb(stB, "wd_sb", [128, 8192], BF16)
            qt_g = sb(stB, "qt_g", [128, 4096], BF16)
            kt_b = sb(stB, "kt_b", [128, 8192], BF16)
            v_b = sb(stB, "v_b", [128, 8192], BF16)
            pt_sb = sb(stB, "pt_sb", [128, 16384], BF16)
            ot_sb = sb(stB, "ot_sb", [128, 4096], BF16)
            xq_sb = sb(stB, "xq_sb", [128, 2048], F32)
            xmid = sb(stB, "xmid", [128, 8192], F32)
            fu_sb = sb(stB, "fu_sb", [128, 4096], BF16)
            xd_sb = sb(stB, "xd_sb", [128, 2048], BF16)
            xdT_sb = sb(stB, "xdT_sb", [128, 4096], BF16)
            out_sb = sb(stB, "out_sb", [128, 1024], F32)

            for eng in ENGINES:
                for sname in ("s_qkw", "s_sc", "s_dv", "s_pe", "s_lx", "s_lw",
                              "s_l1r", "s_l1w", "s_ms"):
                    S.wait(eng, sname, a_finals[sname])

            stage_ev = [None, None]
            wo_cast_done = None
            stg_b = xmid[:, 0:2048]
            for c in range(8):
                par = c % 2
                w = [("s_dv", stage_ev[par])] if stage_ev[par] else []
                ev_l = S.op("sync", lambda e, c=c, par=par: e.dma_start(
                    out=stg_b[:, par * 1024:(par + 1) * 1024],
                    in_=d_woT[c * 128:(c + 1) * 128, :]),
                    inc=("s_lw", 16), waits=w)
                ev_c = vop(lambda e, c=c, par=par: nc.vector.tensor_copy(
                    wo_sb[:, c * 1024:(c + 1) * 1024],
                    stg_b[:, par * 1024:(par + 1) * 1024]),
                    waits=[("s_lw", ev_l["s_lw"])])
                stage_ev[par] = ev_c["s_dv"]
                wo_cast_done = ev_c["s_dv"]
            wd_load_done = None
            for c in range(8):
                ev_t = S.op("sync", lambda e, c=c: e.dma_start(
                    out=wd_sb[:, c * 1024:(c + 1) * 1024],
                    in_=d_wd[:, c * 128:(c + 1) * 128], transpose=True),
                    inc=("s_kvr", 16))
                wd_load_done = ev_t["s_kvr"]

            groups = [(b, g) for b in range(BPC) for g in range(G)]

            def xmb(grp):
                return ((grp[0] * G + grp[1]) % 2) * 4096

            S_stop_ev = {}
            AVchunk_ev = {}
            exp_ev = {}
            qt_load_ev = {}
            kt_load_ev = {}
            v_load_ev = {}
            B_row_ev = {}
            sqb_pe_B = [None, None]
            last_S_ev = [None]
            last_qsq_sc = [None]
            last_Aproj_ev = [None]
            l_read_ev = [None, None]
            l_evac_ev = {}
            lbf_ev = {}
            recip_ev = [None]
            invl_of = {}
            xq_prev = [None, None]
            x_ev = {}
            ln2_ev = {}
            out_dma_ev = [None, None]
            qv_last_ev = [None]
            xtr_ev = {}
            F_ev = {}
            fu_reuse = [None] * NSUB
            prep_ev = {}
            prep_sc = {}
            F_stop = {}
            aproj_stop = {}
            ot_ready = {}
            stats_e1 = {}
            xadd_ev = {}

            def emit_qt_load(grp):
                b, g = grp
                w = []
                if last_S_ev[0]:
                    w.append(("s_pe", last_S_ev[0]))
                if last_qsq_sc[0]:
                    w.append(("s_sc", last_qsq_sc[0]))
                ev = S.op("gpsimd", lambda e, b=b, g=g: e.dma_start(
                    out=qt_g[:].rearrange("p (c f) -> p c f", f=512),
                    in_=d_qt[b, g].rearrange("c p f -> p c f")),
                    inc=("s_kvr", 16), waits=w)
                qt_load_ev[grp] = ev["s_kvr"]

            def emit_ktload(grp, j, wait_ev):
                b, g = grp
                par = j % 2
                w = [("s_pe", wait_ev)] if wait_ev else []
                ev = S.op("gpsimd", lambda e, b=b, j=j, par=par: e.dma_start(
                    out=kt_b[:, par * 4096:(par + 1) * 4096]
                        .rearrange("p (c f) -> p c f", f=512),
                    in_=d_kt[b, j].rearrange("c p f -> p c f")),
                    inc=("s_kvr", 16), waits=w)
                kt_load_ev[(grp, j)] = ev["s_kvr"]

            def emit_vload(grp, j, wait_ev):
                b, g = grp
                par = j % 2
                w = [("s_pe", wait_ev)] if wait_ev else []
                ev = S.op("sync", lambda e, b=b, j=j, par=par: e.dma_start(
                    out=v_b[:, par * 4096:(par + 1) * 4096]
                        .rearrange("p (t d) -> p t d", d=1024),
                    in_=d_v[b, j].rearrange("t p d -> p t d")),
                    inc=("s_kvr", 16), waits=w)
                v_load_ev[(grp, j)] = ev["s_kvr"]

            def emit_xload(grp, s_i):
                b2, g2 = grp
                xpar = s_i % 2
                r0 = b2 * SEQ + g2 * GW + s_i * 128
                w = []
                if xq_prev[xpar] is not None:
                    w.append(("s_dv", xq_prev[xpar]))
                ev = S.op("gpsimd", lambda e, r0=r0, xpar=xpar:
                          e.dma_start(
                              out=xq_sb[:, xpar * 1024:(xpar + 1) * 1024],
                              in_=d_x[r0:r0 + 128, :]),
                          inc=("s_lx", 16), waits=w)
                x_ev[(grp, s_i)] = ev["s_lx"]

            def emit_qnorm(grp):
                b, g = grp
                qn_stop = None
                for m in range(8):
                    qpar = m % 2
                    wsc = [("s_kvr", qt_load_ev[grp])]
                    if sqb_pe_B[qpar]:
                        wsc.append(("s_pe", sqb_pe_B[qpar]))
                    ev_sq = S.op("scalar", lambda e, m=m, qpar=qpar: e.activation(
                        sqb[:, qpar * 512:(qpar + 1) * 512],
                        qt_g[:, m * 512:(m + 1) * 512], AF.Square),
                        inc=("s_sc", 1), waits=wsc)
                    last_qsq_sc[0] = ev_sq["s_sc"]
                    waits = [("s_sc", ev_sq["s_sc"])]
                    if m == 0:
                        waits += bankB_ev[3]
                    ev_mm = S.op("tensor", lambda e, m=m, qpar=qpar:
                                 nc.tensor.matmul(
                                     row_ps, ones_c[:],
                                     sqb[:, qpar * 512:(qpar + 1) * 512],
                                     start=(m == 0), stop=(m == 7)),
                                 inc=("s_pe", 1), waits=waits)
                    sqb_pe_B[qpar] = ev_mm["s_pe"]
                    qn_stop = ev_mm["s_pe"]
                ev_qn = S.op("scalar", lambda e: e.activation(
                    rowf[0:1, 1024:1536], row_ps, AF.Copy),
                    inc=("s_sc", 1), waits=[("s_pe", qn_stop)])
                bankB_ev[3] = [("s_sc", ev_qn["s_sc"])]
                S.op("scalar", lambda e, b=b: e.activation(
                    rowf[0:1, 1024:1536], rowf[0:1, 1024:1536], AF.Identity,
                    scale=kmax[0:1, b:b + 1], bias=kmax[0:1, 2:3]))
                wsc = []
                if last_S_ev[0]:
                    wsc.append(("s_pe", last_S_ev[0]))
                S.op("scalar", lambda e: e.activation(
                    rowf[0:1, 512:1024], rowf[0:1, 1024:1536], AF.Ln), waits=wsc)
                S.op("scalar", lambda e: e.activation(
                    rowf[0:1, 512:1024], rowf[0:1, 512:1024], AF.Exp, scale=0.5))
                ev_B = S.op("scalar", lambda e: e.activation(
                    rowbf[0:1, 512:1024], rowf[0:1, 512:1024], AF.Copy),
                    inc=("s_sc", 1))
                B_row_ev[grp] = ev_B["s_sc"]

            emit_qt_load(groups[0])
            emit_ktload(groups[0], 0, None)
            emit_ktload(groups[0], 1, None)
            emit_vload(groups[0], 0, None)
            emit_vload(groups[0], 1, None)
            emit_qnorm(groups[0])

            NIT = len(groups)
            for idx in range(NIT + 2):
                cur = groups[idx] if idx < NIT else None
                prev = groups[idx - 1] if 1 <= idx <= NIT else None
                nxt = groups[idx + 1] if idx + 1 < NIT else None
                p2 = groups[idx - 2] if 2 <= idx <= NIT + 1 else None

                if prev is not None:
                    xb1 = xmb(prev)
                    g3 = groups[idx - 3] if idx >= 3 else None
                    ps4 = [psB[:, 0:1024], psB[:, 1024:2048],
                           psA[:, 0:1024], psA[:, 1024:2048]]
                    e1a_ev = {}
                    for s_i in (0, 1, 2, 3):
                        w = [("s_pe", aproj_stop[(prev, s_i)]),
                             ("s_dv", invl_of[prev])]
                        if g3 is not None:
                            w.append(("s_sc", prep_sc[(g3, s_i)]))
                        ev_c = vop(lambda e, s_i=s_i, ps=ps4[s_i], xb1=xb1:
                                   nc.vector.tensor_scalar(
                                       xmid[:, xb1 + s_i * 1024:
                                            xb1 + (s_i + 1) * 1024], ps,
                                       invl[:, s_i:s_i + 1], None, ALU.mult),
                                   waits=w)
                        e1a_ev[s_i] = ev_c["s_dv"]
                    bankA_ev[0] = [("s_dv", e1a_ev[2])]
                    bankA_ev[1] = [("s_dv", e1a_ev[2])]
                    bankA_ev[2] = [("s_dv", e1a_ev[3])]
                    bankA_ev[3] = [("s_dv", e1a_ev[3])]
                    bankB_ev[0] = [("s_dv", e1a_ev[0])]
                    bankB_ev[1] = [("s_dv", e1a_ev[0])]
                    bankB_ev[2] = [("s_dv", e1a_ev[1])]
                    bankB_ev[3] = [("s_dv", e1a_ev[1])]

                    def emit_E1b_pair(pair, prev=prev, xb1=xb1):
                        srcs = [xmid[:, xb1 + s * 1024:xb1 + (s + 1) * 1024]
                                for s in pair]
                        wdvl = [[("s_dv", e1a_ev[s])] for s in pair]
                        stats = ln_stats4(srcs, 1024, 0 + pair[0], wdvl, wdvl)
                        for k, s_i in enumerate(pair):
                            rstd, nb, ev_st = stats[k]
                            ev_ln2 = S.op("scalar", lambda e, s_i=s_i, xb1=xb1,
                                          rstd=rstd, nb=nb: e.activation(
                                              xmid[:, xb1 + s_i * 1024:
                                                   xb1 + (s_i + 1) * 1024],
                                              xmid[:, xb1 + s_i * 1024:
                                                   xb1 + (s_i + 1) * 1024],
                                              AF.Identity,
                                              scale=rstd, bias=nb),
                                          inc=("s_sc", 1),
                                          waits=[("s_dv", ev_st)])
                            ln2_ev[(prev, s_i)] = ev_ln2["s_sc"]
                            xpar = s_i % 2
                            ev_xm = vop(lambda e, s_i=s_i, xpar=xpar, xb1=xb1:
                                        nc.vector.tensor_tensor(
                                            xmid[:, xb1 + s_i * 1024:
                                                 xb1 + (s_i + 1) * 1024],
                                            xmid[:, xb1 + s_i * 1024:
                                                 xb1 + (s_i + 1) * 1024],
                                            xq_sb[:, xpar * 1024:
                                                  (xpar + 1) * 1024],
                                            ALU.add),
                                        waits=[("s_sc", ev_ln2["s_sc"]),
                                               ("s_lx", x_ev[(prev, s_i)])])
                            xadd_ev[(prev, s_i)] = ev_xm["s_dv"]
                            xq_prev[xpar] = ev_xm["s_dv"]
                            if s_i + 2 < NSUB:
                                emit_xload(prev, s_i + 2)

                    def emit_prep_pair(pair, prev=prev, xb1=xb1):
                        srcs3 = [xmid[:, xb1 + s * 1024:xb1 + (s + 1) * 1024]
                                 for s in pair]
                        wdvl = [[("s_dv", xadd_ev[(prev, s)])] for s in pair]
                        stats3 = ln_stats4(srcs3, 1024, 4 + pair[0], wdvl, wdvl)
                        for k, s_i in enumerate(pair):
                            cols = slot(4 + s_i)
                            S.op("vector", lambda e, cols=cols:
                                 nc.vector.tensor_single_scalar(
                                     cols["scl"], cols["rstd"], -1.0, ALU.mult))
                            ev_n = vop(lambda e, cols=cols:
                                       nc.vector.tensor_single_scalar(
                                           cols["msq"], cols["nb"], -1.0,
                                           ALU.mult))
                            prep_ev[(prev, s_i)] = ev_n["s_dv"]
                            prep_sc[(prev, s_i)] = S.cnt["s_sc"]

                    def emit_E2a_sub(s_i, prev=prev, p2=p2, xb1=xb1):
                        lpar3 = s_i % 2
                        cols = slot(4 + s_i)
                        qw = [("s_dv", prep_ev[(prev, s_i)]),
                              ("s_sc", prep_sc[(prev, s_i)])]
                        pk = (prev, s_i - 2) if s_i > 1 else \
                            ((p2, NSUB - 2 + s_i) if p2 else None)
                        if pk is not None and pk in xtr_ev:
                            qw.append(("s_xtr", xtr_ev[pk]))
                        ev_xd = emit_quant2(
                            xmid[:, xb1 + s_i * 1024:xb1 + (s_i + 1) * 1024],
                            cols["rstd"], cols["nb"], cols["scl"], cols["msq"],
                            xd_sb[:, lpar3 * 1024:(lpar3 + 1) * 1024],
                            waits_dv=qw)
                        qv_last_ev[0] = ev_xd
                        w = [("s_dv", ev_xd)]
                        if p2 is not None and (p2, s_i) in F_stop:
                            w.append(("s_pe", F_stop[(p2, s_i)]))
                        for c in range(8):
                            ev_tp = S.op("sync", lambda e, c=c, s_i=s_i,
                                         lpar3=lpar3: e.dma_start(
                                             out=xdT_sb[:, s_i * 1024 + c * 128:
                                                        s_i * 1024 +
                                                        (c + 1) * 128],
                                             in_=xd_sb[:, lpar3 * 1024 + c * 128:
                                                       lpar3 * 1024 +
                                                       (c + 1) * 128],
                                             transpose=True),
                                         inc=("s_xtr", 16), waits=w)
                            w = []
                        xtr_ev[(prev, s_i)] = ev_tp["s_xtr"]

                if p2 is not None:
                    def emit_F(s_i, p2=p2):
                        bidx = (0, 1) if s_i % 2 == 0 else (2, 3)
                        ev_mm = None
                        for half in range(2):
                            bkp = bankB[bidx[half]]
                            waits = [("s_xtr", xtr_ev[(p2, s_i)])] + \
                                bankB_ev[bidx[half]] + [("s_dv", bias_bf_ev)]
                            for c in range(8):
                                S.op("tensor", lambda e, c=c, s_i=s_i, bkp=bkp,
                                     half=half: nc.tensor.matmul(
                                         bkp,
                                         xdT_sb[:, s_i * 1024 + c * 128:
                                                s_i * 1024 + (c + 1) * 128],
                                         wd_sb[:, c * 1024 + half * 512:
                                               c * 1024 + (half + 1) * 512],
                                         start=(c == 0), stop=False),
                                     waits=waits if c == 0 else ())
                                waits = ()
                            ev_mm = S.op("tensor", lambda e, half=half, bkp=bkp:
                                         nc.tensor.matmul(
                                             bkp, ones_r[:],
                                             bfrow[0:1, half * 512:
                                                   (half + 1) * 512],
                                             start=False, stop=True),
                                         inc=("s_pe", 1))
                        F_stop[(p2, s_i)] = ev_mm["s_pe"]
                        wfu = [("s_pe", ev_mm["s_pe"])]
                        if fu_reuse[s_i] is not None:
                            wfu.append(("s_sc", fu_reuse[s_i]))
                        ev_fe1 = vop(lambda e, s_i=s_i: nc.vector.tensor_copy(
                            fu_sb[:, s_i * 1024:s_i * 1024 + 512],
                            bankB[bidx[0]]), waits=wfu)
                        ev_fe2 = S.op("scalar", lambda e, s_i=s_i: e.activation(
                            fu_sb[:, s_i * 1024 + 512:(s_i + 1) * 1024],
                            bankB[bidx[1]], AF.Copy), inc=("s_sc", 1),
                            waits=wfu)
                        bankB_ev[bidx[0]] = [("s_dv", ev_fe1["s_dv"])]
                        bankB_ev[bidx[1]] = [("s_sc", ev_fe2["s_sc"])]
                        F_ev[(p2, s_i)] = [("s_dv", ev_fe1["s_dv"]),
                                           ("s_sc", ev_fe2["s_sc"])]

                    def emit_E2b_pair(pair, p2=p2):
                        xb2 = xmb(p2)
                        srcs4 = [fu_sb[:, s_i * 1024:(s_i + 1) * 1024]
                                 for s_i in pair]
                        wdvl = [F_ev[(p2, s_i)] for s_i in pair]
                        stats4 = ln_stats4(srcs4, 1024, 8 + pair[0], wdvl, wdvl)
                        for k, s_i in enumerate(pair):
                            scl4, nb4, ev4 = stats4[k]
                            wsc = [("s_dv", ev4)]
                            if out_dma_ev[0]:
                                wsc.append(("s_ow", out_dma_ev[0]))
                            ev_l4 = S.op("scalar", lambda e, k=k,
                                         scl4=scl4, nb4=nb4, srcs4=srcs4:
                                         e.activation(
                                             out_sb[:],
                                             srcs4[k], AF.Identity,
                                             scale=scl4, bias=nb4),
                                         inc=("s_sc", 1), waits=wsc)
                            fu_reuse[s_i] = ev_l4["s_sc"]
                            ev_oa = vop(lambda e, s_i=s_i, xb2=xb2:
                                        nc.vector.tensor_tensor(
                                            out_sb[:], out_sb[:],
                                            xmid[:, xb2 + s_i * 1024:
                                                 xb2 + (s_i + 1) * 1024],
                                            ALU.add),
                                        waits=[("s_sc", ev_l4["s_sc"])])
                            pb, pg = p2
                            r0 = pb * SEQ + pg * GW + s_i * 128
                            ev_od = S.op("sync", lambda e, r0=r0:
                                         e.dma_start(
                                             out=d_out[r0:r0 + 128, :],
                                             in_=out_sb[:]),
                                         inc=("s_ow", 16),
                                         waits=[("s_dv", ev_oa["s_dv"])])
                            out_dma_ev[0] = ev_od["s_ow"]

                    emit_F(0)
                    emit_F(1)
                    emit_F(2)
                    emit_F(3)

                if cur is None:
                    if p2 is not None:
                        emit_E2b_pair([0, 1])
                        emit_E2b_pair([2, 3])
                    if prev is not None:
                        emit_E1b_pair([0, 1])
                        emit_E1b_pair([2, 3])
                        emit_prep_pair([0, 1])
                        emit_prep_pair([2, 3])
                        for s_i in range(NSUB):
                            emit_E2a_sub(s_i)

                if cur is not None:
                    b, g = cur
                    pe_base = [("s_sc", B_row_ev[cur]),
                               ("s_dv", wo_cast_done or 0),
                               ("s_kvr", wd_load_done)]
                    for j in range(8):
                        par = j % 2
                        for kt4 in range(4):
                            ktile = j * 4 + kt4
                            bk_i = ktile % 4
                            waits = [("s_kvr", kt_load_ev[(cur, j)])] + \
                                list(pe_base) + bankA_ev[bk_i]
                            pkey = (prev, ktile // 4) if prev else None
                            for c in range(8):
                                S.op("tensor", lambda e, j=j, kt4=kt4, c=c,
                                     par=par, bk_i=bk_i: nc.tensor.matmul(
                                         bankA[bk_i],
                                         kt_b[:, par * 4096 + c * 512 +
                                              kt4 * 128:
                                              par * 4096 + c * 512 +
                                              (kt4 + 1) * 128],
                                         qt_g[:, c * 512:(c + 1) * 512],
                                         start=(c == 0), stop=False),
                                     waits=waits if c == 0 else ())
                                waits = ()
                            ev_mm = S.op("tensor", lambda e, bk_i=bk_i:
                                         nc.tensor.matmul(
                                             bankA[bk_i], nones_r[:],
                                             rowbf[0:1, 512:1024],
                                             start=False, stop=True),
                                         inc=("s_pe", 1))
                            S_stop_ev[(cur, ktile)] = ev_mm["s_pe"]
                            wsc = [("s_pe", ev_mm["s_pe"])]
                            if pkey is not None and pkey in AVchunk_ev:
                                wsc.append(("s_pe", AVchunk_ev[pkey]))
                            ev_e = S.op("scalar", lambda e, ktile=ktile,
                                        bk_i=bk_i: e.activation(
                                            pt_sb[:, ktile * 512:
                                                  (ktile + 1) * 512],
                                            bankA[bk_i], AF.Exp),
                                        inc=("s_sc", 1), waits=wsc)
                            exp_ev[(cur, ktile)] = ev_e["s_sc"]
                            bankA_ev[bk_i] = [("s_sc", ev_e["s_sc"])]
                        if j + 2 < 8:
                            emit_ktload(cur, j + 2, S_stop_ev[(cur, j * 4 + 3)])
                        if p2 is not None:
                            if j == 0:
                                emit_E2b_pair([0, 1])
                            elif j == 1:
                                emit_E2b_pair([2, 3])
                        if prev is not None:
                            if j == 2:
                                emit_E1b_pair([0, 1])
                            elif j == 3:
                                emit_E1b_pair([2, 3])
                            elif j == 4:
                                emit_prep_pair([0, 1])
                            elif j == 5:
                                emit_prep_pair([2, 3])
                            elif j == 6:
                                emit_E2a_sub(0)
                                emit_E2a_sub(1)
                            elif j == 7:
                                emit_E2a_sub(2)
                                emit_E2a_sub(3)
                    last_S_ev[0] = S_stop_ev[(cur, 31)]
                    if nxt is not None:
                        emit_qt_load(nxt)
                        emit_ktload(nxt, 0, S_stop_ev[(cur, 27)])
                        emit_ktload(nxt, 1, S_stop_ev[(cur, 31)])

                    waits = [("s_sc", exp_ev[(cur, 31)])] + bankB_ev[3]
                    for ktile in range(32):
                        ev_mm = S.op("tensor", lambda e, ktile=ktile:
                                     nc.tensor.matmul(
                                         row_ps, ones_c[:],
                                         pt_sb[:, ktile * 512:(ktile + 1) * 512],
                                         start=(ktile == 0), stop=(ktile == 31)),
                                     inc=("s_pe", 1) if ktile == 31 else None,
                                     waits=waits if ktile == 0 else ())
                        waits = ()
                    wsc = [("s_pe", ev_mm["s_pe"])]
                    if last_Aproj_ev[0]:
                        wsc.append(("s_pe", last_Aproj_ev[0]))
                    ev_l = S.op("scalar", lambda e: e.activation(
                        rowf[0:1, 0:512], row_ps, AF.Copy),
                        inc=("s_sc", 1), waits=wsc)
                    l_evac_ev[cur] = ev_l["s_sc"]
                    ev_lbf = S.op("scalar", lambda e: e.activation(
                        rowbf[0:1, 0:512], rowf[0:1, 0:512], AF.Copy),
                        inc=("s_sc", 1))
                    lbf_ev[cur] = ev_lbf["s_sc"]
                    bankB_ev[3] = [("s_sc", ev_l["s_sc"])]
                    lpar = g % 2
                    w = [("s_sc", ev_l["s_sc"])]
                    if l_read_ev[lpar]:
                        w.append(("s_ldr", l_read_ev[lpar]))
                    ev_lw = S.op("gpsimd", lambda e, lpar=lpar: e.dma_start(
                        out=d_l[lpar:lpar + 1, :], in_=rowf[0:1, 0:512]),
                        inc=("s_ldr", 16), waits=w)
                    w = [("s_ldr", ev_lw["s_ldr"])]
                    if recip_ev[0]:
                        w.append(("s_dv", recip_ev[0]))
                    ev_lr = S.op("gpsimd", lambda e, lpar=lpar: e.dma_start(
                        out=lt[:], in_=d_l[lpar].rearrange("(c p) -> p c", p=128),
                        allow_slow_non_contiguous=True),
                        inc=("s_ldr", 16), waits=w)
                    l_read_ev[lpar] = ev_lr["s_ldr"]
                    ev_rc = vop(lambda e: nc.vector.reciprocal(invl[:], lt[:]),
                                waits=[("s_ldr", ev_lr["s_ldr"])])
                    recip_ev[0] = ev_rc["s_dv"]
                    invl_of[cur] = ev_rc["s_dv"]

                    if nxt is not None:
                        emit_qnorm(nxt)


                if cur is not None:
                    started = [False] * 8
                    for j in range(8):
                        par = j % 2
                        for t4 in range(4):
                            ktile = j * 4 + t4
                            for c in range(8):
                                bk = bankA[c] if c < 4 else bankB[c - 4]
                                waits = []
                                if not started[c]:
                                    waits = [("s_kvr", v_load_ev[(cur, j)]),
                                             ("s_sc", exp_ev[(cur, j * 4 + 3)])]
                                    waits += (bankA_ev[c] if c < 4
                                              else bankB_ev[c - 4])
                                    started[c] = True
                                elif t4 == 0 and c == 0:
                                    waits = [("s_kvr", v_load_ev[(cur, j)]),
                                             ("s_sc", exp_ev[(cur, j * 4 + 3)])]
                                ev_mm = S.op("tensor", lambda e, j=j, t4=t4, c=c,
                                             par=par, bk=bk, ktile=ktile:
                                             nc.tensor.matmul(
                                                 bk,
                                                 v_b[:, par * 4096 + t4 * 1024 +
                                                     c * 128:
                                                     par * 4096 + t4 * 1024 +
                                                     (c + 1) * 128],
                                                 pt_sb[:, ktile * 512:
                                                       (ktile + 1) * 512],
                                                 start=(ktile == 0),
                                                 stop=(ktile == 31)),
                                             inc=("s_pe", 1) if
                                             (ktile == 31 or (t4 == 3 and c == 7))
                                             else None,
                                             waits=waits)
                        AVchunk_ev[(cur, j)] = ev_mm["s_pe"]
                        if j + 2 < 8:
                            emit_vload(cur, j + 2, AVchunk_ev[(cur, j)])
                    if nxt is not None:
                        emit_vload(nxt, 0, AVchunk_ev[(cur, 6)])
                        emit_vload(nxt, 1, AVchunk_ev[(cur, 7)])
                    av_last = AVchunk_ev[(cur, 7)]
                    w_extra = ([("s_pe", last_Aproj_ev[0])]
                               if last_Aproj_ev[0] else [])
                    ot_ready[cur] = [None] * 8
                    for c in range(4):
                        ev_o = vop(lambda e, c=c: nc.vector.tensor_copy(
                            ot_sb[:, c * 512:(c + 1) * 512], bankA[c]),
                            waits=[("s_pe", av_last)] + (w_extra if c == 0 else []))
                        ot_ready[cur][c] = ("s_dv", ev_o["s_dv"])
                        bankA_ev[c] = [("s_dv", ev_o["s_dv"])]
                    for c in range(4, 8):
                        ev_o = S.op("scalar", lambda e, c=c: e.activation(
                            ot_sb[:, c * 512:(c + 1) * 512], bankB[c - 4],
                            AF.Copy), inc=("s_sc", 1),
                            waits=[("s_pe", av_last)] +
                                  (w_extra if c == 4 else []))
                        ot_ready[cur][c] = ("s_sc", ev_o["s_sc"])
                        bankB_ev[c - 4] = [("s_sc", ev_o["s_sc"])]

                    apj_banks = [(bankB, 0, 1), (bankB, 2, 3),
                                 (bankA, 0, 1), (bankA, 2, 3)]

                    def emit_aproj(s_i):
                        bks, i0, i1 = apj_banks[s_i]
                        bev = bankB_ev if bks is bankB else bankA_ev
                        ev_mm = None
                        for half, bi in ((0, i0), (1, i1)):
                            bkp = bks[bi]
                            first_w = [("s_sc", lbf_ev[cur]),
                                       ("s_dv", bias_bf_ev)] + bev[bi]
                            for c in range(8):
                                w = list(first_w) + [ot_ready[cur][c]]
                                first_w = []
                                S.op("tensor", lambda e, c=c, s_i=s_i, bkp=bkp,
                                     half=half: nc.tensor.matmul(
                                         bkp,
                                         ot_sb[:, c * 512 + s_i * 128:
                                               c * 512 + (s_i + 1) * 128],
                                         wo_sb[:, c * 1024 + half * 512:
                                               c * 1024 + (half + 1) * 512],
                                         start=(c == 0), stop=False),
                                     waits=w)
                            ev_mm = S.op("tensor", lambda e, s_i=s_i, bkp=bkp,
                                         half=half: nc.tensor.matmul(
                                             bkp,
                                             rowbf[0:1, s_i * 128:
                                                   (s_i + 1) * 128],
                                             borow[0:1, half * 512:
                                                   (half + 1) * 512],
                                             start=False, stop=True),
                                         inc=("s_pe", 1))
                        aproj_stop[(cur, s_i)] = ev_mm["s_pe"]
                        last_Aproj_ev[0] = ev_mm["s_pe"]

                    emit_xload(cur, 0)
                    emit_xload(cur, 1)
                    emit_aproj(0)
                    emit_aproj(1)
                    emit_aproj(2)
                    emit_aproj(3)


            S.wait("gpsimd", "s_ow", S.cnt["s_ow"])

            with nc.Block() as blockB:
                S.flush(blockB)

    return nc


def kernel(**inputs):
    x = np.asarray(inputs["L0_attn_x"])
    wq = np.asarray(inputs["L0_attn_Wq"])
    wk = np.asarray(inputs["L0_attn_Wk"])
    wv = np.asarray(inputs["L0_attn_Wv"])
    wo = np.asarray(inputs["L0_attn_Wo"])
    bq = np.asarray(inputs["L0_attn_bq"])
    bk = np.asarray(inputs["L0_attn_bk"])
    bv = np.asarray(inputs["L0_attn_bv"])
    bo = np.asarray(inputs["L0_attn_bo"])
    wf = np.asarray(inputs["L0_ffn_weight"])
    bf = np.asarray(inputs["L0_ffn_bias"])

    wqT = np.ascontiguousarray(wq.T)
    wkT = np.ascontiguousarray(wk.T)
    wvT = np.ascontiguousarray(wv.T)
    woT = np.ascontiguousarray(wo.T)
    bql = np.ascontiguousarray(bq.reshape(8, 128).T)
    bkl = np.ascontiguousarray(bk.reshape(8, 128).T)
    bvr = np.ascontiguousarray(bv.reshape(1, H))
    bor = np.ascontiguousarray(bo.reshape(1, H))
    bfr = np.ascontiguousarray(bf.reshape(1, H))

    nc = build_nc()
    in_maps = []
    for i in range(NCORES):
        xs = np.ascontiguousarray(x[BPC * i:BPC * (i + 1)].reshape(BPC * SEQ, H))
        in_maps.append({
            "x": xs, "wqT": wqT, "wkT": wkT, "wvT": wvT, "woT": woT,
            "wf": np.ascontiguousarray(wf),
            "bql": bql, "bkl": bkl, "bvr": bvr, "bor": bor, "bfr": bfr,
        })
    res = run_bass_kernel_spmd(nc, in_maps, core_ids=list(range(NCORES)))
    out = np.stack([np.asarray(r["out"]) for r in res.results])
    return out.reshape(16, SEQ, H).astype(np.float32, copy=False)


# version 27
# speedup vs baseline: 1.2360x; 1.2360x over previous
import numpy as np
import ml_dtypes
import concourse.bass as bass
import concourse.mybir as mybir
from concourse.bass_utils import run_bass_kernel_spmd
from contextlib import ExitStack

F32 = mybir.dt.float32
BF16 = mybir.dt.bfloat16
I32 = mybir.dt.int32
AF = mybir.ActivationFunctionType
ALU = mybir.AluOpType
AX = mybir.AxisListType

NCORES = 8
BPC = 2
SEQ = 4096
H = 1024
G = 8
GW = 512
NSUB = 4
QSC = float(1.0 / np.sqrt(512.0))
EPS = 1e-5

ENGINES = ("sync", "gpsimd", "vector", "scalar", "tensor")


class Sched:
    def __init__(self, sems):
        self.sems = sems
        self.cnt = {k: 0 for k in sems}
        self.ops = {e: [] for e in ENGINES}
        self.waited = {e: {k: 0 for k in sems} for e in ENGINES}

    def wait(self, eng, name, val):
        if val is None or val <= 0:
            return
        if self.waited[eng][name] >= val:
            return
        self.waited[eng][name] = val
        sem = self.sems[name]
        self.ops[eng].append(lambda e, sem=sem, val=val: e.wait_ge(sem, val))

    def op(self, eng, build, inc=None, waits=()):
        for w in waits:
            if w is not None:
                self.wait(eng, w[0], w[1])
        incs = []
        ret = {}
        if inc is not None:
            if isinstance(inc, tuple):
                inc = [inc]
            for (name, k) in inc:
                self.cnt[name] += k
                incs.append((self.sems[name], k))
                ret[name] = self.cnt[name]
        def run(e, build=build, incs=tuple(incs)):
            inst = build(e)
            for (sem, k) in incs:
                inst = inst.then_inc(sem, k)
        self.ops[eng].append(run)
        return ret

    def flush(self, block):
        for eng in ENGINES:
            ops = self.ops[eng]
            def body(e, ops=ops):
                for f in ops:
                    f(e)
            getattr(block, eng)(body)
        self.ops = {e: [] for e in ENGINES}


def build_nc(dump=False):
    nc = bass.Bass("TRN2", target_bir_lowering=False, debug=False)

    d_x = nc.dram_tensor("x", [BPC * SEQ, H], F32, kind="ExternalInput").ap()
    d_wqT = nc.dram_tensor("wqT", [H, H], F32, kind="ExternalInput").ap()
    d_wkT = nc.dram_tensor("wkT", [H, H], F32, kind="ExternalInput").ap()
    d_wvT = nc.dram_tensor("wvT", [H, H], F32, kind="ExternalInput").ap()
    d_woT = nc.dram_tensor("woT", [H, H], F32, kind="ExternalInput").ap()
    d_wf = nc.dram_tensor("wf", [H, H], F32, kind="ExternalInput").ap()
    d_bql = nc.dram_tensor("bql", [128, 8], F32, kind="ExternalInput").ap()
    d_bkl = nc.dram_tensor("bkl", [128, 8], F32, kind="ExternalInput").ap()
    d_bvr = nc.dram_tensor("bvr", [1, H], F32, kind="ExternalInput").ap()
    d_bor = nc.dram_tensor("bor", [1, H], F32, kind="ExternalInput").ap()
    d_bfr = nc.dram_tensor("bfr", [1, H], F32, kind="ExternalInput").ap()
    d_out = nc.dram_tensor("out", [BPC * SEQ, H], F32, kind="ExternalOutput").ap()

    scr = "ExternalOutput" if dump else "Internal"
    d_qt = nc.dram_tensor("qt_d", [BPC, G, 8, 128, GW], BF16, kind=scr).ap()
    d_kt = nc.dram_tensor("kt_d", [BPC, G, 8, 128, GW], BF16, kind=scr).ap()
    d_v = nc.dram_tensor("v_d", [BPC, G, 4, 128, H], BF16, kind=scr).ap()
    d_wd = nc.dram_tensor("wd_d", [H, H], BF16, kind=scr).ap()
    d_ln1 = nc.dram_tensor("ln1_d", [2, GW, H], BF16, kind="Internal").ap()
    d_l = nc.dram_tensor("l_d", [2, GW], F32, kind="Internal").ap()

    root = ExitStack()
    with root:
        semnames = ["s_lx", "s_lw", "s_ms", "s_l1w", "s_l1r", "s_qkw", "s_kvr",
                    "s_ldr", "s_xtr", "s_ow", "s_pe", "s_sc", "s_dv"]
        sems = {n: root.enter_context(nc.semaphore(n)) for n in semnames}
        S = Sched(sems)

        def sb(st, name, shape, dt):
            return st.enter_context(nc.sbuf_tensor(name, shape, dt))

        bql = sb(root, "bql_s", [128, 8], F32)
        bkl = sb(root, "bkl_s", [128, 8], F32)
        rowf = sb(root, "rowf", [1, 1544], F32)
        rowbf = sb(root, "rowbf", [1, 1024], BF16)
        biasrow = sb(root, "biasrow", [1, 3072], BF16)
        kmax = sb(root, "kmax", [1, 3], F32)
        lt = sb(root, "l_t", [128, 4], F32)
        invl = sb(root, "invl", [128, 4], F32)
        ones_c = sb(root, "ones_c", [128, 1], BF16)
        ones_r = sb(root, "ones_r", [1, 128], BF16)
        nones_r = sb(root, "nones_r", [1, 128], BF16)
        st_ = sb(root, "stats", [128, 128], F32)
        sqsc = sb(root, "sqsc", [128, 1024], BF16)
        sqb = sb(root, "sqb", [128, 1024], BF16)
        qv = sb(root, "qv", [128, 1024], F32)
        qa = sb(root, "qa", [128, 1024], F32)
        qg2 = sb(root, "qg2", [128, 1024], BF16)
        qam = sb(root, "qam", [128, 32], F32)
        qsb = sb(root, "qsb", [128, 32], F32)
        qis = sb(root, "qis", [128, 32], F32)

        psA = root.enter_context(nc.psum_tensor("psA", [128, 2048], F32))
        psB = root.enter_context(nc.psum_tensor("psB", [128, 2048], F32))
        bankA = [psA[:, i * 512:(i + 1) * 512] for i in range(4)]
        bankB = [psB[:, i * 512:(i + 1) * 512] for i in range(4)]
        row_ps = psB[0:1, 1536:2048]
        bankA_ev = [[] for _ in range(4)]
        bankB_ev = [[] for _ in range(4)]

        def vop(build, waits=()):
            ev = S.op("vector", build, inc=("s_dv", 1), waits=waits)
            S.wait("vector", "s_dv", ev["s_dv"])
            return ev

        def slot(idx):
            base = idx * 8
            return {k: st_[:, base + i:base + i + 1] for i, k in enumerate(
                ["sum", "ssq", "mean", "msq", "var", "rstd", "nb", "scl"])}

        def ln_stats4(srcs, n, slot_base, waits_dv, waits_sc, extra_scales=None,
                      dve_sq=False):
            ns = len(srcs)
            C = [slot(slot_base + i) for i in range(ns)]
            ev_red = [None] * ns
            for i in range(ns):
                ev_red[i] = vop(lambda e, i=i: nc.vector.tensor_reduce(
                    C[i]["sum"], srcs[i], AX.X, ALU.add),
                    waits=waits_dv[i] if waits_dv else ())
            ev_sq = [None] * ns
            if dve_sq:
                for i in range(ns):
                    vop(lambda e, i=i: nc.vector.tensor_tensor(
                        qa[:, 0:n], srcs[i], srcs[i], ALU.mult))
                    ev_sq[i] = vop(lambda e, i=i: nc.vector.tensor_reduce(
                        C[i]["ssq"], qa[:, 0:n], AX.X, ALU.add))
            else:
                for i in range(ns):
                    ev_sq[i] = S.op("scalar", lambda e, i=i: e.activation(
                        sqsc[:, 0:n], srcs[i], AF.Square, accum_out=C[i]["ssq"]),
                        inc=("s_sc", 1),
                        waits=(list(waits_sc[i]) if waits_sc else []) +
                              [("s_dv", ev_red[i]["s_dv"])])
            for i in range(ns):
                wsq = [] if dve_sq else [("s_sc", ev_sq[i]["s_sc"])]
                xs = extra_scales[i] if extra_scales else None
                if xs is not None:
                    vop(lambda e, i=i, xs=xs: nc.vector.tensor_scalar(
                        C[i]["mean"], C[i]["sum"], 1.0 / n, xs, ALU.mult, ALU.mult),
                        waits=wsq)
                    vop(lambda e, i=i, xs=xs: nc.vector.tensor_scalar(
                        C[i]["ssq"], C[i]["ssq"], xs, xs, ALU.mult, ALU.mult))
                else:
                    vop(lambda e, i=i: nc.vector.tensor_single_scalar(
                        C[i]["mean"], C[i]["sum"], 1.0 / n, ALU.mult),
                        waits=wsq)
            for i in range(ns):
                vop(lambda e, i=i: nc.vector.tensor_tensor(
                    C[i]["msq"], C[i]["mean"], C[i]["mean"], ALU.mult))
            for i in range(ns):
                vop(lambda e, i=i: nc.vector.scalar_tensor_tensor(
                    C[i]["var"], C[i]["ssq"], 1.0 / n, C[i]["msq"],
                    ALU.mult, ALU.subtract))
            ev_var = [None] * ns
            for i in range(ns):
                ev_var[i] = vop(lambda e, i=i: nc.vector.tensor_single_scalar(
                    C[i]["var"], C[i]["var"], EPS, ALU.add))
            for i in range(ns):
                S.op("scalar", lambda e, i=i: e.activation(
                    C[i]["msq"], C[i]["var"], AF.Ln),
                    waits=[("s_dv", ev_var[i]["s_dv"])])
            ev_r = [None] * ns
            for i in range(ns):
                ev_r[i] = S.op("scalar", lambda e, i=i: e.activation(
                    C[i]["rstd"], C[i]["msq"], AF.Exp, scale=-0.5),
                    inc=("s_sc", 1))
            out = []
            for i in range(ns):
                xs = extra_scales[i] if extra_scales else None
                if xs is not None:
                    vop(lambda e, i=i, xs=xs: nc.vector.tensor_tensor(
                        C[i]["scl"], C[i]["rstd"], xs, ALU.mult),
                        waits=[("s_sc", ev_r[i]["s_sc"])])
                ev = vop(lambda e, i=i: nc.vector.tensor_scalar(
                    C[i]["nb"], C[i]["mean"], C[i]["rstd"], -1.0,
                    ALU.mult, ALU.mult), waits=[("s_sc", ev_r[i]["s_sc"])])
                out.append((C[i]["scl"] if xs is not None else C[i]["rstd"],
                            C[i]["nb"], ev["s_dv"]))
            return out

        def emit_quant(src_ap, out_ap, waits_dv=()):
            src3 = src_ap.rearrange("p (b i) -> p b i", i=32)
            vop(lambda e: nc.vector.tensor_reduce(
                qam[:], src3, AX.X, ALU.max, apply_absolute_value=True),
                waits=waits_dv)
            vop(lambda e: nc.vector.tensor_scalar(
                qsb[:], qam[:], 1.0 / 7.5, 1e-30, ALU.mult, ALU.max))
            vop(lambda e: nc.vector.reciprocal(qis[:], qsb[:]))
            isb = qis[:].unsqueeze(2).broadcast_to([128, 32, 32])
            ssb = qsb[:].unsqueeze(2).broadcast_to([128, 32, 32])
            ev_v = vop(lambda e: nc.vector.tensor_tensor(
                qv[:].rearrange("p (b i) -> p b i", i=32), src3, isb, ALU.mult))
            ev_a = S.op("scalar", lambda e: e.activation(qa[:], qv[:], AF.Abs),
                        inc=("s_sc", 1), waits=[("s_dv", ev_v["s_dv"])])
            vop(lambda e: nc.vector.tensor_single_scalar(
                qg2[:], qa[:], 2.0, ALU.is_ge), waits=[("s_sc", ev_a["s_sc"])])
            vop(lambda e: nc.vector.scalar_tensor_tensor(
                qg2[:], qa[:], 4.0, qg2[:], ALU.is_ge, ALU.add))
            vop(lambda e: nc.vector.scalar_tensor_tensor(
                qa[:], qg2[:], -5.0, qg2[:], ALU.add, ALU.mult))
            vop(lambda e: nc.vector.tensor_single_scalar(
                qa[:], qa[:], 8.0, ALU.add))
            vop(lambda e: nc.vector.tensor_tensor(qv[:], qv[:], qa[:], ALU.mult))
            vop(lambda e: nc.vector.tensor_single_scalar(
                qv[:], qv[:], 12582912.0, ALU.add))
            vop(lambda e: nc.vector.tensor_single_scalar(
                qv[:], qv[:], -12582912.0, ALU.add))
            vop(lambda e: nc.vector.scalar_tensor_tensor(
                qa[:], qg2[:], 1.0, qg2[:], ALU.add, ALU.mult))
            vop(lambda e: nc.vector.tensor_scalar(
                qa[:], qa[:], 0.0625, 0.125, ALU.mult, ALU.add))
            vop(lambda e: nc.vector.tensor_tensor(qv[:], qv[:], qa[:], ALU.mult))
            ev = vop(lambda e: nc.vector.tensor_tensor(
                out_ap.rearrange("p (b i) -> p b i", i=32),
                qv[:].rearrange("p (b i) -> p b i", i=32), ssb, ALU.mult))
            return ev["s_dv"]

        def emit_quant2(xm, rstd, nb3, nrstd, nnb, out_ap, waits_dv=()):
            xm3 = xm.rearrange("p (b i) -> p b i", i=32)
            ev1 = S.op("vector", lambda e: nc.vector.tensor_reduce(
                qam[:], xm3, AX.X, ALU.max), inc=("s_dv", 1), waits=waits_dv)
            ev2 = S.op("vector", lambda e: nc.vector.tensor_reduce(
                qsb[:], xm3, AX.X, ALU.min), inc=("s_dv", 1))
            S.wait("vector", "s_dv", ev2["s_dv"])
            vop(lambda e: nc.vector.tensor_scalar(
                qam[:], qam[:], rstd, nb3, ALU.mult, ALU.add))
            vop(lambda e: nc.vector.tensor_scalar(
                qsb[:], qsb[:], nrstd, nnb, ALU.mult, ALU.add))
            vop(lambda e: nc.vector.tensor_tensor(qam[:], qam[:], qsb[:], ALU.max))
            vop(lambda e: nc.vector.tensor_scalar(
                qsb[:], qam[:], 1.0 / 7.5, 1e-30, ALU.mult, ALU.max))
            vop(lambda e: nc.vector.reciprocal(qis[:], qsb[:]))
            vop(lambda e: nc.vector.tensor_scalar(
                qam[:], qis[:], rstd, None, ALU.mult))
            ev3 = vop(lambda e: nc.vector.tensor_scalar(
                qis[:], qis[:], nb3, None, ALU.mult))
            rsb = qam[:].unsqueeze(2).broadcast_to([128, 32, 32])
            mbb = qis[:].unsqueeze(2).broadcast_to([128, 32, 32])
            ssb = qsb[:].unsqueeze(2).broadcast_to([128, 32, 32])
            S.op("vector", lambda e: nc.vector.tensor_tensor(
                qv[:].rearrange("p (b i) -> p b i", i=32), xm3, rsb, ALU.mult))
            S.op("vector", lambda e: nc.vector.tensor_tensor(
                qv[:].rearrange("p (b i) -> p b i", i=32),
                qv[:].rearrange("p (b i) -> p b i", i=32), mbb, ALU.add))
            S.op("vector", lambda e: nc.vector.tensor_tensor(
                qa[:], qv[:], qv[:], ALU.mult))
            S.op("vector", lambda e: nc.vector.tensor_single_scalar(
                qg2[:], qa[:], 4.0, ALU.is_ge))
            S.op("vector", lambda e: nc.vector.scalar_tensor_tensor(
                qg2[:], qa[:], 16.0, qg2[:], ALU.is_ge, ALU.add))
            S.op("vector", lambda e: nc.vector.scalar_tensor_tensor(
                qa[:], qg2[:], -5.0, qg2[:], ALU.add, ALU.mult))
            S.op("vector", lambda e: nc.vector.tensor_single_scalar(
                qa[:], qa[:], 8.0, ALU.add))
            S.op("vector", lambda e: nc.vector.tensor_tensor(
                qv[:], qv[:], qa[:], ALU.mult))
            S.op("vector", lambda e: nc.vector.tensor_single_scalar(
                qv[:], qv[:], 12582912.0, ALU.add))
            S.op("vector", lambda e: nc.vector.tensor_single_scalar(
                qv[:], qv[:], -12582912.0, ALU.add))
            S.op("vector", lambda e: nc.vector.scalar_tensor_tensor(
                qa[:], qg2[:], 1.0, qg2[:], ALU.add, ALU.mult))
            S.op("vector", lambda e: nc.vector.tensor_scalar(
                qa[:], qa[:], 0.0625, 0.125, ALU.mult, ALU.add))
            S.op("vector", lambda e: nc.vector.tensor_tensor(
                qv[:], qv[:], qa[:], ALU.mult))
            ev = S.op("vector", lambda e: nc.vector.tensor_tensor(
                out_ap.rearrange("p (b i) -> p b i", i=32),
                qv[:].rearrange("p (b i) -> p b i", i=32), ssb, ALU.mult),
                inc=("s_dv", 1))
            S.wait("vector", "s_dv", ev["s_dv"])
            return ev["s_dv"]

        with ExitStack() as stA:
            stage = sb(stA, "stage", [128, 2048], F32)
            wq_sb = sb(stA, "wq_sb", [128, 8192], BF16)
            wk_sb = sb(stA, "wk_sb", [128, 8192], BF16)
            wv_sb = sb(stA, "wv_sb", [128, 8192], BF16)
            x_sb = sb(stA, "x_sbA", [128, 8192], F32)
            ln1_sb = sb(stA, "ln1_sb", [128, 8192], BF16)
            ln1T_sb = sb(stA, "ln1T_sb", [128, 8192], BF16)
            qt_st = sb(stA, "qt_st", [128, 4096], BF16)
            kt_st = sb(stA, "kt_st", [128, 4096], BF16)
            v_st = sb(stA, "v_st", [128, 4096], BF16)
            wd_st = sb(stA, "wd_st", [128, 2048], BF16)
            rowstg = sb(stA, "rowstg", [1, 1024], F32)
            sqkn = sb(stA, "sqkn", [128, 4096], BF16)

            S.op("gpsimd", lambda e: e.dma_start(out=bql[:], in_=d_bql[:]),
                 inc=("s_ms", 16))
            S.op("gpsimd", lambda e: e.dma_start(out=bkl[:], in_=d_bkl[:]),
                 inc=("s_ms", 16))
            ev_ms = None
            for i, dsrc in enumerate([d_bvr, d_bor, d_bfr]):
                ev_ms = S.op("gpsimd", lambda e, dsrc=dsrc: e.dma_start(
                    out=rowstg[:], in_=dsrc[:]), inc=("s_ms", 16),
                    waits=[("s_dv", ev_cast["s_dv"])] if i else [])
                ev_cast = vop(lambda e, i=i: nc.vector.tensor_copy(
                    biasrow[0:1, i * 1024:(i + 1) * 1024], rowstg[:]),
                    waits=[("s_ms", ev_ms["s_ms"])])
            bias_bf_ev = ev_cast["s_dv"]
            bvrow = biasrow[0:1, 0:1024]
            borow = biasrow[0:1, 1024:2048]
            bfrow = biasrow[0:1, 2048:3072]
            vop(lambda e: nc.vector.memset(ones_c[:], 1.0))
            vop(lambda e: nc.vector.memset(ones_r[:], 1.0))
            vop(lambda e: nc.vector.memset(nones_r[:], -1.0))
            vop(lambda e: nc.vector.memset(kmax[:], 0.0))
            ev_bqs = vop(lambda e: nc.vector.tensor_single_scalar(
                bql[:], bql[:], QSC, ALU.mult), waits=[("s_ms", 32)])

            stage_ev = [None, None]
            cast_done = None
            for wi, (dsrc, dst) in enumerate([(d_wqT, wq_sb), (d_wkT, wk_sb),
                                              (d_wvT, wv_sb)]):
                for c in range(8):
                    par = (wi * 8 + c) % 2
                    w = [("s_dv", stage_ev[par])] if stage_ev[par] else []
                    ev_l = S.op("sync", lambda e, dsrc=dsrc, c=c, par=par:
                                e.dma_start(
                                    out=stage[:, par * 1024:(par + 1) * 1024],
                                    in_=dsrc[c * 128:(c + 1) * 128, :]),
                                inc=("s_lw", 16), waits=w)
                    ev_c = vop(lambda e, dst=dst, c=c, par=par:
                               nc.vector.tensor_copy(
                                   dst[:, c * 1024:(c + 1) * 1024],
                                   stage[:, par * 1024:(par + 1) * 1024]),
                               waits=[("s_lw", ev_l["s_lw"])])
                    stage_ev[par] = ev_c["s_dv"]
                    cast_done = ev_c["s_dv"]

            wd_wr_ev = [None, None]
            for c in range(8):
                par = c % 2
                w = [("s_dv", stage_ev[par])] if stage_ev[par] else []
                ev_l = S.op("sync", lambda e, c=c, par=par: e.dma_start(
                    out=stage[:, par * 1024:(par + 1) * 1024],
                    in_=d_wf[c * 128:(c + 1) * 128, :]),
                    inc=("s_lw", 16), waits=w)
                qw = [("s_lw", ev_l["s_lw"])]
                if wd_wr_ev[par]:
                    qw.append(("s_qkw", wd_wr_ev[par]))
                ev_q = emit_quant(stage[:, par * 1024:(par + 1) * 1024],
                                  wd_st[:, par * 1024:(par + 1) * 1024],
                                  waits_dv=qw)
                stage_ev[par] = ev_q
                ev_w = S.op("gpsimd", lambda e, c=c, par=par: e.dma_start(
                    out=d_wd[c * 128:(c + 1) * 128, :],
                    in_=wd_st[:, par * 1024:(par + 1) * 1024]),
                    inc=("s_qkw", 16), waits=[("s_dv", ev_q)])
                wd_wr_ev[par] = ev_w["s_qkw"]

            a_apply_ev = [None, None]
            a_red_ev = [None, None]
            a_l1w_ev = [None, None]
            a_l1r_ev = [None, None]
            a_vmm_ev = [None, None]
            a_qkw_ev = None
            sqb_pe_ev = [None]

            l1r_of = {}

            def emit_ln1(gi):
                b2, g2 = divmod(gi, G)
                par = gi % 2
                r0 = b2 * SEQ + g2 * GW
                w = []
                if a_apply_ev[par]:
                    w += [("s_sc", a_apply_ev[par]), ("s_dv", a_red_ev[par])]
                ev_x = S.op("gpsimd", lambda e, r0=r0, par=par: e.dma_start(
                    out=x_sb[:, par * 4096:(par + 1) * 4096]
                        .rearrange("p (s d) -> p s d", d=1024),
                    in_=d_x[r0:r0 + GW, :].rearrange("(s p) d -> p s d", p=128)),
                    inc=("s_lx", 16), waits=w)
                srcs = [x_sb[:, par * 4096 + s * 1024:
                             par * 4096 + (s + 1) * 1024] for s in range(4)]
                stats = ln_stats4(
                    srcs, 1024, 12,
                    [[("s_lx", ev_x["s_lx"])]] * 4,
                    [[("s_lx", ev_x["s_lx"])]] * 4)
                a_red_ev[par] = S.cnt["s_dv"]
                wsc = [("s_l1w", a_l1w_ev[par])] if a_l1w_ev[par] else []
                last_apply = None
                for s in range(NSUB):
                    scl, nb, ev_st = stats[s]
                    ev_ap = S.op("scalar", lambda e, s=s, par=par, scl=scl,
                                 nb=nb, src_=srcs[s]: e.activation(
                                     ln1_sb[:, par * 4096 + s * 1024:
                                            par * 4096 + (s + 1) * 1024],
                                     src_, AF.Identity, scale=scl, bias=nb),
                                 inc=("s_sc", 1),
                                 waits=[("s_dv", ev_st)] + wsc)
                    wsc = []
                    last_apply = ev_ap["s_sc"]
                a_apply_ev[par] = last_apply
                w = [("s_sc", last_apply)]
                if a_l1r_ev[par]:
                    w.append(("s_l1r", a_l1r_ev[par]))
                ev_w = S.op("gpsimd", lambda e, par=par: e.dma_start(
                    out=d_ln1[par].rearrange("(s p) d -> p s d", p=128),
                    in_=ln1_sb[:, par * 4096:(par + 1) * 4096]
                        .rearrange("p (s d) -> p s d", d=1024)),
                    inc=("s_l1w", 16), waits=w)
                a_l1w_ev[par] = ev_w["s_l1w"]
                w = [("s_l1w", ev_w["s_l1w"])]
                if a_vmm_ev[par]:
                    w.append(("s_pe", a_vmm_ev[par]))
                ev_r = None
                for c in range(8):
                    ev_r = S.op("sync", lambda e, c=c, par=par: e.dma_start(
                        out=ln1T_sb[:, par * 4096 + c * 512:
                                    par * 4096 + (c + 1) * 512],
                        in_=d_ln1[par][:, c * 128:(c + 1) * 128],
                        transpose=True),
                        inc=("s_l1r", 16), waits=w)
                    w = []
                a_l1r_ev[par] = ev_r["s_l1r"]
                l1r_of[gi] = ev_r["s_l1r"]

            emit_ln1(0)
            for gi in range(BPC * G):
                b, g = divmod(gi, G)
                if True:
                    par = gi % 2
                    if gi + 1 < BPC * G:
                        emit_ln1(gi + 1)
                    l1r_done = l1r_of[gi]

                    pe_w = [("s_l1r", l1r_done), ("s_dv", cast_done),
                            ("s_dv", ev_bqs["s_dv"]), ("s_dv", bias_bf_ev),
                            ("s_ms", 32)]
                    for which, (wsb, bias_ap, sc_const) in enumerate(
                            [(wq_sb, bql, QSC), (wk_sb, bkl, 1.0)]):
                        for m in range(8):
                            bk_i = m % 4
                            waits = list(pe_w) + bankA_ev[bk_i]
                            for c in range(8):
                                ev_mm = S.op("tensor", lambda e, wsb=wsb, c=c,
                                             m=m, par=par, bk_i=bk_i:
                                             nc.tensor.matmul(
                                                 bankA[bk_i],
                                                 wsb[:, c * 1024 + m * 128:
                                                     c * 1024 + (m + 1) * 128],
                                                 ln1T_sb[:, par * 4096 + c * 512:
                                                         par * 4096 + (c + 1) * 512],
                                                 start=(c == 0), stop=(c == 7)),
                                             inc=("s_pe", 1) if c == 7 else None,
                                             waits=waits if c == 0 else ())
                                waits = ()
                            dst = qt_st if which == 0 else kt_st
                            wsc = [("s_pe", ev_mm["s_pe"])]
                            if a_qkw_ev:
                                wsc.append(("s_qkw", a_qkw_ev))
                            ev_ev = S.op("scalar", lambda e, dst=dst, m=m,
                                         bias_ap=bias_ap, sc_const=sc_const,
                                         bk_i=bk_i: e.activation(
                                             dst[:, m * 512:(m + 1) * 512],
                                             bankA[bk_i], AF.Identity,
                                             scale=sc_const,
                                             bias=bias_ap[:, m:m + 1]),
                                         inc=("s_sc", 1), waits=wsc)
                            bankA_ev[bk_i] = [("s_sc", ev_ev["s_sc"])]
                            if which == 1:
                                wsq = ([("s_pe", sqb_pe_ev[0])]
                                       if (m == 0 and sqb_pe_ev[0]) else [])
                                ev_sqm = S.op("scalar", lambda e, m=m:
                                              e.activation(
                                                  sqkn[:, m * 512:(m + 1) * 512],
                                                  kt_st[:, m * 512:(m + 1) * 512],
                                                  AF.Square),
                                              inc=("s_sc", 1), waits=wsq)
                                kn_sq_ev = ev_sqm["s_sc"]
                    vgrp = 0
                    for t4 in range(4):
                        for half in range(2):
                            bk_i = vgrp % 3
                            vgrp += 1
                            waits = list(pe_w) + bankB_ev[bk_i]
                            for c in range(8):
                                S.op("tensor", lambda e, c=c, t4=t4, half=half,
                                     par=par, bk_i=bk_i: nc.tensor.matmul(
                                         bankB[bk_i],
                                         ln1T_sb[:, par * 4096 + c * 512 +
                                                 t4 * 128:
                                                 par * 4096 + c * 512 +
                                                 (t4 + 1) * 128],
                                         wv_sb[:, c * 1024 + half * 512:
                                               c * 1024 + (half + 1) * 512],
                                         start=(c == 0), stop=False),
                                     waits=waits if c == 0 else ())
                                waits = ()
                            ev_mm = S.op("tensor", lambda e, half=half, bk_i=bk_i:
                                         nc.tensor.matmul(
                                             bankB[bk_i], ones_r[:],
                                             bvrow[0:1, half * 512:(half + 1) * 512],
                                             start=False, stop=True),
                                         inc=("s_pe", 1))
                            wdv = [("s_pe", ev_mm["s_pe"])]
                            if a_qkw_ev:
                                wdv.append(("s_qkw", a_qkw_ev))
                            ev_ev = vop(lambda e, t4=t4, half=half, bk_i=bk_i:
                                        nc.vector.tensor_copy(
                                            v_st[:, t4 * 1024 + half * 512:
                                                 t4 * 1024 + (half + 1) * 512],
                                            bankB[bk_i]), waits=wdv)
                            bankB_ev[bk_i] = [("s_dv", ev_ev["s_dv"])]
                    for m in range(8):
                        waits = [("s_sc", kn_sq_ev)] if m == 0 else []
                        if m == 0:
                            waits += bankB_ev[3]
                        ev_mm = S.op("tensor", lambda e, m=m: nc.tensor.matmul(
                            row_ps, ones_c[:],
                            sqkn[:, m * 512:(m + 1) * 512],
                            start=(m == 0), stop=(m == 7)),
                            inc=("s_pe", 1) if m == 7 else None, waits=waits)
                    sqb_pe_ev[0] = ev_mm["s_pe"]
                    ev_kn = S.op("scalar", lambda e: e.activation(
                        rowf[0:1, 1024:1536], row_ps, AF.Copy),
                        inc=("s_sc", 1), waits=[("s_pe", sqb_pe_ev[0])])
                    bankB_ev[3] = [("s_sc", ev_kn["s_sc"])]
                    ev_km = vop(lambda e: nc.vector.tensor_reduce(
                        rowf[0:1, 1536:1537], rowf[0:1, 1024:1536], AX.X, ALU.max),
                        waits=[("s_sc", ev_kn["s_sc"])])
                    vop(lambda e, b=b: nc.vector.tensor_tensor(
                        kmax[0:1, b:b + 1], kmax[0:1, b:b + 1],
                        rowf[0:1, 1536:1537], ALU.max))

                    wq_ = [("s_sc", S.cnt["s_sc"]), ("s_dv", ev_ev["s_dv"])]
                    S.op("sync", lambda e, b=b, g=g: e.dma_start(
                        out=d_qt[b, g].rearrange("c p f -> p c f"),
                        in_=qt_st[:].rearrange("p (c f) -> p c f", f=512)),
                        inc=("s_qkw", 16), waits=wq_)
                    S.op("sync", lambda e, b=b, g=g: e.dma_start(
                        out=d_kt[b, g].rearrange("c p f -> p c f"),
                        in_=kt_st[:].rearrange("p (c f) -> p c f", f=512)),
                        inc=("s_qkw", 16))
                    ev3 = S.op("sync", lambda e, b=b, g=g: e.dma_start(
                        out=d_v[b, g].rearrange("t p d -> p t d"),
                        in_=v_st[:].rearrange("p (t d) -> p t d", d=1024)),
                        inc=("s_qkw", 16))
                    a_qkw_ev = ev3["s_qkw"]
                    a_vmm_ev[par] = ev_mm["s_pe"]

            a_finals = dict(S.cnt)
            with nc.Block() as blockA:
                S.flush(blockA)

        with ExitStack() as stB:
            wo_sb = sb(stB, "wo_sb", [128, 8192], BF16)
            wd_sb = sb(stB, "wd_sb", [128, 8192], BF16)
            qt_g = sb(stB, "qt_g", [128, 4096], BF16)
            kt_b = sb(stB, "kt_b", [128, 8192], BF16)
            v_b = sb(stB, "v_b", [128, 8192], BF16)
            pt_sb = sb(stB, "pt_sb", [128, 16384], BF16)
            ot_sb = sb(stB, "ot_sb", [128, 4096], BF16)
            xq_sb = sb(stB, "xq_sb", [128, 2048], F32)
            xmid = sb(stB, "xmid", [128, 8192], F32)
            fu_sb = sb(stB, "fu_sb", [128, 4096], BF16)
            xd_sb = sb(stB, "xd_sb", [128, 2048], BF16)
            xdT_sb = sb(stB, "xdT_sb", [128, 4096], BF16)
            out_sb = sb(stB, "out_sb", [128, 1024], F32)

            for eng in ENGINES:
                for sname in ("s_qkw", "s_sc", "s_dv", "s_pe", "s_lx", "s_lw",
                              "s_l1r", "s_l1w", "s_ms"):
                    S.wait(eng, sname, a_finals[sname])

            stage_ev = [None, None]
            wo_cast_done = None
            stg_b = xmid[:, 0:2048]
            for c in range(8):
                par = c % 2
                w = [("s_dv", stage_ev[par])] if stage_ev[par] else []
                ev_l = S.op("sync", lambda e, c=c, par=par: e.dma_start(
                    out=stg_b[:, par * 1024:(par + 1) * 1024],
                    in_=d_woT[c * 128:(c + 1) * 128, :]),
                    inc=("s_lw", 16), waits=w)
                ev_c = vop(lambda e, c=c, par=par: nc.vector.tensor_copy(
                    wo_sb[:, c * 1024:(c + 1) * 1024],
                    stg_b[:, par * 1024:(par + 1) * 1024]),
                    waits=[("s_lw", ev_l["s_lw"])])
                stage_ev[par] = ev_c["s_dv"]
                wo_cast_done = ev_c["s_dv"]
            wd_load_done = None
            for c in range(8):
                ev_t = S.op("sync", lambda e, c=c: e.dma_start(
                    out=wd_sb[:, c * 1024:(c + 1) * 1024],
                    in_=d_wd[:, c * 128:(c + 1) * 128], transpose=True),
                    inc=("s_kvr", 16))
                wd_load_done = ev_t["s_kvr"]

            groups = [(b, g) for b in range(BPC) for g in range(G)]

            def xmb(grp):
                return ((grp[0] * G + grp[1]) % 2) * 4096

            S_stop_ev = {}
            AVchunk_ev = {}
            exp_ev = {}
            qt_load_ev = {}
            kt_load_ev = {}
            v_load_ev = {}
            B_row_ev = {}
            sqb_pe_B = [None, None]
            last_S_ev = [None]
            last_qsq_sc = [None]
            last_Aproj_ev = [None]
            l_read_ev = [None, None]
            l_evac_ev = {}
            lbf_ev = {}
            recip_ev = [None]
            invl_of = {}
            xq_prev = [None, None]
            x_ev = {}
            ln2_ev = {}
            out_dma_ev = [None, None]
            qv_last_ev = [None]
            xtr_ev = {}
            F_ev = {}
            fu_reuse = [None] * NSUB
            prep_ev = {}
            prep_sc = {}
            F_stop = {}
            aproj_stop = {}
            ot_ready = {}
            stats_e1 = {}
            xadd_ev = {}

            def emit_qt_load(grp):
                b, g = grp
                w = []
                if last_S_ev[0]:
                    w.append(("s_pe", last_S_ev[0]))
                if last_qsq_sc[0]:
                    w.append(("s_sc", last_qsq_sc[0]))
                ev = S.op("gpsimd", lambda e, b=b, g=g: e.dma_start(
                    out=qt_g[:].rearrange("p (c f) -> p c f", f=512),
                    in_=d_qt[b, g].rearrange("c p f -> p c f")),
                    inc=("s_kvr", 16), waits=w)
                qt_load_ev[grp] = ev["s_kvr"]

            def emit_ktload(grp, j, wait_ev):
                b, g = grp
                par = j % 2
                w = [("s_pe", wait_ev)] if wait_ev else []
                ev = S.op("gpsimd", lambda e, b=b, j=j, par=par: e.dma_start(
                    out=kt_b[:, par * 4096:(par + 1) * 4096]
                        .rearrange("p (c f) -> p c f", f=512),
                    in_=d_kt[b, j].rearrange("c p f -> p c f")),
                    inc=("s_kvr", 16), waits=w)
                kt_load_ev[(grp, j)] = ev["s_kvr"]

            def emit_vload(grp, j, wait_ev):
                b, g = grp
                par = j % 2
                w = [("s_pe", wait_ev)] if wait_ev else []
                ev = S.op("gpsimd", lambda e, b=b, j=j, par=par: e.dma_start(
                    out=v_b[:, par * 4096:(par + 1) * 4096]
                        .rearrange("p (t d) -> p t d", d=1024),
                    in_=d_v[b, j].rearrange("t p d -> p t d")),
                    inc=("s_kvr", 16), waits=w)
                v_load_ev[(grp, j)] = ev["s_kvr"]

            def emit_xload(grp, s_i):
                b2, g2 = grp
                xpar = s_i % 2
                r0 = b2 * SEQ + g2 * GW + s_i * 128
                w = []
                if xq_prev[xpar] is not None:
                    w.append(("s_dv", xq_prev[xpar]))
                ev = S.op("gpsimd", lambda e, r0=r0, xpar=xpar:
                          e.dma_start(
                              out=xq_sb[:, xpar * 1024:(xpar + 1) * 1024],
                              in_=d_x[r0:r0 + 128, :]),
                          inc=("s_lx", 16), waits=w)
                x_ev[(grp, s_i)] = ev["s_lx"]

            def emit_qnorm(grp):
                b, g = grp
                qn_stop = None
                for m in range(8):
                    qpar = m % 2
                    wsc = [("s_kvr", qt_load_ev[grp])]
                    if sqb_pe_B[qpar]:
                        wsc.append(("s_pe", sqb_pe_B[qpar]))
                    ev_sq = S.op("scalar", lambda e, m=m, qpar=qpar: e.activation(
                        sqb[:, qpar * 512:(qpar + 1) * 512],
                        qt_g[:, m * 512:(m + 1) * 512], AF.Square),
                        inc=("s_sc", 1), waits=wsc)
                    last_qsq_sc[0] = ev_sq["s_sc"]
                    waits = [("s_sc", ev_sq["s_sc"])]
                    if m == 0:
                        waits += bankB_ev[3]
                    ev_mm = S.op("tensor", lambda e, m=m, qpar=qpar:
                                 nc.tensor.matmul(
                                     row_ps, ones_c[:],
                                     sqb[:, qpar * 512:(qpar + 1) * 512],
                                     start=(m == 0), stop=(m == 7)),
                                 inc=("s_pe", 1), waits=waits)
                    sqb_pe_B[qpar] = ev_mm["s_pe"]
                    qn_stop = ev_mm["s_pe"]
                ev_qn = S.op("scalar", lambda e: e.activation(
                    rowf[0:1, 1024:1536], row_ps, AF.Copy),
                    inc=("s_sc", 1), waits=[("s_pe", qn_stop)])
                bankB_ev[3] = [("s_sc", ev_qn["s_sc"])]
                S.op("scalar", lambda e, b=b: e.activation(
                    rowf[0:1, 1024:1536], rowf[0:1, 1024:1536], AF.Identity,
                    scale=kmax[0:1, b:b + 1], bias=kmax[0:1, 2:3]))
                wsc = []
                if last_S_ev[0]:
                    wsc.append(("s_pe", last_S_ev[0]))
                S.op("scalar", lambda e: e.activation(
                    rowf[0:1, 512:1024], rowf[0:1, 1024:1536], AF.Ln), waits=wsc)
                S.op("scalar", lambda e: e.activation(
                    rowf[0:1, 512:1024], rowf[0:1, 512:1024], AF.Exp, scale=0.5))
                ev_B = S.op("scalar", lambda e: e.activation(
                    rowbf[0:1, 512:1024], rowf[0:1, 512:1024], AF.Copy),
                    inc=("s_sc", 1))
                B_row_ev[grp] = ev_B["s_sc"]

            emit_qt_load(groups[0])
            emit_ktload(groups[0], 0, None)
            emit_ktload(groups[0], 1, None)
            emit_vload(groups[0], 0, None)
            emit_vload(groups[0], 1, None)
            emit_qnorm(groups[0])

            NIT = len(groups)
            for idx in range(NIT + 2):
                cur = groups[idx] if idx < NIT else None
                prev = groups[idx - 1] if 1 <= idx <= NIT else None
                nxt = groups[idx + 1] if idx + 1 < NIT else None
                p2 = groups[idx - 2] if 2 <= idx <= NIT + 1 else None

                if prev is not None:
                    xb1 = xmb(prev)
                    g3 = groups[idx - 3] if idx >= 3 else None
                    ps4 = [psB[:, 0:1024], psB[:, 1024:2048],
                           psA[:, 0:1024], psA[:, 1024:2048]]
                    e1a_ev = {}
                    for s_i in (0, 1, 2, 3):
                        w = [("s_pe", aproj_stop[(prev, s_i)]),
                             ("s_dv", invl_of[prev])]
                        if g3 is not None:
                            w.append(("s_sc", prep_sc[(g3, s_i)]))
                        ev_c = vop(lambda e, s_i=s_i, ps=ps4[s_i], xb1=xb1:
                                   nc.vector.tensor_scalar(
                                       xmid[:, xb1 + s_i * 1024:
                                            xb1 + (s_i + 1) * 1024], ps,
                                       invl[:, s_i:s_i + 1], None, ALU.mult),
                                   waits=w)
                        e1a_ev[s_i] = ev_c["s_dv"]
                    bankA_ev[0] = [("s_dv", e1a_ev[2])]
                    bankA_ev[1] = [("s_dv", e1a_ev[2])]
                    bankA_ev[2] = [("s_dv", e1a_ev[3])]
                    bankA_ev[3] = [("s_dv", e1a_ev[3])]
                    bankB_ev[0] = [("s_dv", e1a_ev[0])]
                    bankB_ev[1] = [("s_dv", e1a_ev[0])]
                    bankB_ev[2] = [("s_dv", e1a_ev[1])]
                    bankB_ev[3] = [("s_dv", e1a_ev[1])]

                    def emit_E1b_pair(pair, prev=prev, xb1=xb1):
                        srcs = [xmid[:, xb1 + s * 1024:xb1 + (s + 1) * 1024]
                                for s in pair]
                        wdvl = [[("s_dv", e1a_ev[s])] for s in pair]
                        stats = ln_stats4(srcs, 1024, 0 + pair[0], wdvl, wdvl)
                        for k, s_i in enumerate(pair):
                            rstd, nb, ev_st = stats[k]
                            ev_ln2 = S.op("scalar", lambda e, s_i=s_i, xb1=xb1,
                                          rstd=rstd, nb=nb: e.activation(
                                              xmid[:, xb1 + s_i * 1024:
                                                   xb1 + (s_i + 1) * 1024],
                                              xmid[:, xb1 + s_i * 1024:
                                                   xb1 + (s_i + 1) * 1024],
                                              AF.Identity,
                                              scale=rstd, bias=nb),
                                          inc=("s_sc", 1),
                                          waits=[("s_dv", ev_st)])
                            ln2_ev[(prev, s_i)] = ev_ln2["s_sc"]
                            xpar = s_i % 2
                            ev_xm = vop(lambda e, s_i=s_i, xpar=xpar, xb1=xb1:
                                        nc.vector.tensor_tensor(
                                            xmid[:, xb1 + s_i * 1024:
                                                 xb1 + (s_i + 1) * 1024],
                                            xmid[:, xb1 + s_i * 1024:
                                                 xb1 + (s_i + 1) * 1024],
                                            xq_sb[:, xpar * 1024:
                                                  (xpar + 1) * 1024],
                                            ALU.add),
                                        waits=[("s_sc", ev_ln2["s_sc"]),
                                               ("s_lx", x_ev[(prev, s_i)])])
                            xadd_ev[(prev, s_i)] = ev_xm["s_dv"]
                            xq_prev[xpar] = ev_xm["s_dv"]
                            if s_i + 2 < NSUB:
                                emit_xload(prev, s_i + 2)

                    def emit_prep_pair(pair, prev=prev, xb1=xb1):
                        srcs3 = [xmid[:, xb1 + s * 1024:xb1 + (s + 1) * 1024]
                                 for s in pair]
                        wdvl = [[("s_dv", xadd_ev[(prev, s)])] for s in pair]
                        stats3 = ln_stats4(srcs3, 1024, 4 + pair[0], wdvl, wdvl)
                        for k, s_i in enumerate(pair):
                            cols = slot(4 + s_i)
                            S.op("vector", lambda e, cols=cols:
                                 nc.vector.tensor_single_scalar(
                                     cols["scl"], cols["rstd"], -1.0, ALU.mult))
                            ev_n = vop(lambda e, cols=cols:
                                       nc.vector.tensor_single_scalar(
                                           cols["msq"], cols["nb"], -1.0,
                                           ALU.mult))
                            prep_ev[(prev, s_i)] = ev_n["s_dv"]
                            prep_sc[(prev, s_i)] = S.cnt["s_sc"]

                    def emit_E2a_sub(s_i, prev=prev, p2=p2, xb1=xb1):
                        lpar3 = s_i % 2
                        cols = slot(4 + s_i)
                        qw = [("s_dv", prep_ev[(prev, s_i)]),
                              ("s_sc", prep_sc[(prev, s_i)])]
                        pk = (prev, s_i - 2) if s_i > 1 else \
                            ((p2, NSUB - 2 + s_i) if p2 else None)
                        if pk is not None and pk in xtr_ev:
                            qw.append(("s_xtr", xtr_ev[pk]))
                        ev_xd = emit_quant2(
                            xmid[:, xb1 + s_i * 1024:xb1 + (s_i + 1) * 1024],
                            cols["rstd"], cols["nb"], cols["scl"], cols["msq"],
                            xd_sb[:, lpar3 * 1024:(lpar3 + 1) * 1024],
                            waits_dv=qw)
                        qv_last_ev[0] = ev_xd
                        w = [("s_dv", ev_xd)]
                        if p2 is not None and (p2, s_i) in F_stop:
                            w.append(("s_pe", F_stop[(p2, s_i)]))
                        for c in range(8):
                            ev_tp = S.op("sync", lambda e, c=c, s_i=s_i,
                                         lpar3=lpar3: e.dma_start(
                                             out=xdT_sb[:, s_i * 1024 + c * 128:
                                                        s_i * 1024 +
                                                        (c + 1) * 128],
                                             in_=xd_sb[:, lpar3 * 1024 + c * 128:
                                                       lpar3 * 1024 +
                                                       (c + 1) * 128],
                                             transpose=True),
                                         inc=("s_xtr", 16), waits=w)
                            w = []
                        xtr_ev[(prev, s_i)] = ev_tp["s_xtr"]

                if p2 is not None:
                    def emit_F(s_i, p2=p2):
                        bidx = (0, 1) if s_i % 2 == 0 else (2, 3)
                        ev_mm = None
                        for half in range(2):
                            bkp = bankB[bidx[half]]
                            waits = [("s_xtr", xtr_ev[(p2, s_i)])] + \
                                bankB_ev[bidx[half]] + [("s_dv", bias_bf_ev)]
                            for c in range(8):
                                S.op("tensor", lambda e, c=c, s_i=s_i, bkp=bkp,
                                     half=half: nc.tensor.matmul(
                                         bkp,
                                         xdT_sb[:, s_i * 1024 + c * 128:
                                                s_i * 1024 + (c + 1) * 128],
                                         wd_sb[:, c * 1024 + half * 512:
                                               c * 1024 + (half + 1) * 512],
                                         start=(c == 0), stop=False),
                                     waits=waits if c == 0 else ())
                                waits = ()
                            ev_mm = S.op("tensor", lambda e, half=half, bkp=bkp:
                                         nc.tensor.matmul(
                                             bkp, ones_r[:],
                                             bfrow[0:1, half * 512:
                                                   (half + 1) * 512],
                                             start=False, stop=True),
                                         inc=("s_pe", 1))
                        F_stop[(p2, s_i)] = ev_mm["s_pe"]
                        wfu = [("s_pe", ev_mm["s_pe"])]
                        if fu_reuse[s_i] is not None:
                            wfu.append(("s_sc", fu_reuse[s_i]))
                        ev_fe1 = vop(lambda e, s_i=s_i: nc.vector.tensor_copy(
                            fu_sb[:, s_i * 1024:s_i * 1024 + 512],
                            bankB[bidx[0]]), waits=wfu)
                        ev_fe2 = S.op("scalar", lambda e, s_i=s_i: e.activation(
                            fu_sb[:, s_i * 1024 + 512:(s_i + 1) * 1024],
                            bankB[bidx[1]], AF.Copy), inc=("s_sc", 1),
                            waits=wfu)
                        bankB_ev[bidx[0]] = [("s_dv", ev_fe1["s_dv"])]
                        bankB_ev[bidx[1]] = [("s_sc", ev_fe2["s_sc"])]
                        F_ev[(p2, s_i)] = [("s_dv", ev_fe1["s_dv"]),
                                           ("s_sc", ev_fe2["s_sc"])]

                    def emit_E2b_pair(pair, p2=p2):
                        xb2 = xmb(p2)
                        srcs4 = [fu_sb[:, s_i * 1024:(s_i + 1) * 1024]
                                 for s_i in pair]
                        wdvl = [F_ev[(p2, s_i)] for s_i in pair]
                        stats4 = ln_stats4(srcs4, 1024, 8 + pair[0], wdvl, wdvl)
                        for k, s_i in enumerate(pair):
                            scl4, nb4, ev4 = stats4[k]
                            wsc = [("s_dv", ev4)]
                            if out_dma_ev[0]:
                                wsc.append(("s_ow", out_dma_ev[0]))
                            ev_l4 = S.op("scalar", lambda e, k=k,
                                         scl4=scl4, nb4=nb4, srcs4=srcs4:
                                         e.activation(
                                             out_sb[:],
                                             srcs4[k], AF.Identity,
                                             scale=scl4, bias=nb4),
                                         inc=("s_sc", 1), waits=wsc)
                            fu_reuse[s_i] = ev_l4["s_sc"]
                            ev_oa = vop(lambda e, s_i=s_i, xb2=xb2:
                                        nc.vector.tensor_tensor(
                                            out_sb[:], out_sb[:],
                                            xmid[:, xb2 + s_i * 1024:
                                                 xb2 + (s_i + 1) * 1024],
                                            ALU.add),
                                        waits=[("s_sc", ev_l4["s_sc"])])
                            pb, pg = p2
                            r0 = pb * SEQ + pg * GW + s_i * 128
                            ev_od = S.op("sync", lambda e, r0=r0:
                                         e.dma_start(
                                             out=d_out[r0:r0 + 128, :],
                                             in_=out_sb[:]),
                                         inc=("s_ow", 16),
                                         waits=[("s_dv", ev_oa["s_dv"])])
                            out_dma_ev[0] = ev_od["s_ow"]

                    emit_F(0)
                    emit_F(1)
                    emit_F(2)
                    emit_F(3)

                if cur is None:
                    if p2 is not None:
                        emit_E2b_pair([0, 1])
                        emit_E2b_pair([2, 3])
                    if prev is not None:
                        emit_E1b_pair([0, 1])
                        emit_E1b_pair([2, 3])
                        emit_prep_pair([0, 1])
                        emit_prep_pair([2, 3])
                        for s_i in range(NSUB):
                            emit_E2a_sub(s_i)

                if cur is not None:
                    b, g = cur
                    pe_base = [("s_sc", B_row_ev[cur]),
                               ("s_dv", wo_cast_done or 0),
                               ("s_kvr", wd_load_done)]
                    for j in range(8):
                        par = j % 2
                        for kt4 in range(4):
                            ktile = j * 4 + kt4
                            bk_i = ktile % 4
                            waits = [("s_kvr", kt_load_ev[(cur, j)])] + \
                                list(pe_base) + bankA_ev[bk_i]
                            pkey = (prev, ktile // 4) if prev else None
                            for c in range(8):
                                S.op("tensor", lambda e, j=j, kt4=kt4, c=c,
                                     par=par, bk_i=bk_i: nc.tensor.matmul(
                                         bankA[bk_i],
                                         kt_b[:, par * 4096 + c * 512 +
                                              kt4 * 128:
                                              par * 4096 + c * 512 +
                                              (kt4 + 1) * 128],
                                         qt_g[:, c * 512:(c + 1) * 512],
                                         start=(c == 0), stop=False),
                                     waits=waits if c == 0 else ())
                                waits = ()
                            ev_mm = S.op("tensor", lambda e, bk_i=bk_i:
                                         nc.tensor.matmul(
                                             bankA[bk_i], nones_r[:],
                                             rowbf[0:1, 512:1024],
                                             start=False, stop=True),
                                         inc=("s_pe", 1))
                            S_stop_ev[(cur, ktile)] = ev_mm["s_pe"]
                            wsc = [("s_pe", ev_mm["s_pe"])]
                            if pkey is not None and pkey in AVchunk_ev:
                                wsc.append(("s_pe", AVchunk_ev[pkey]))
                            ev_e = S.op("scalar", lambda e, ktile=ktile,
                                        bk_i=bk_i: e.activation(
                                            pt_sb[:, ktile * 512:
                                                  (ktile + 1) * 512],
                                            bankA[bk_i], AF.Exp),
                                        inc=("s_sc", 1), waits=wsc)
                            exp_ev[(cur, ktile)] = ev_e["s_sc"]
                            bankA_ev[bk_i] = [("s_sc", ev_e["s_sc"])]
                        if j + 2 < 8:
                            emit_ktload(cur, j + 2, S_stop_ev[(cur, j * 4 + 3)])
                        if p2 is not None:
                            if j == 0:
                                emit_E2b_pair([0, 1])
                            elif j == 1:
                                emit_E2b_pair([2, 3])
                        if prev is not None:
                            if j == 2:
                                emit_E1b_pair([0, 1])
                            elif j == 3:
                                emit_E1b_pair([2, 3])
                            elif j == 4:
                                emit_prep_pair([0, 1])
                            elif j == 5:
                                emit_prep_pair([2, 3])
                            elif j == 6:
                                emit_E2a_sub(0)
                                emit_E2a_sub(1)
                            elif j == 7:
                                emit_E2a_sub(2)
                                emit_E2a_sub(3)
                    last_S_ev[0] = S_stop_ev[(cur, 31)]
                    if nxt is not None:
                        emit_qt_load(nxt)
                        emit_ktload(nxt, 0, S_stop_ev[(cur, 27)])
                        emit_ktload(nxt, 1, S_stop_ev[(cur, 31)])

                    waits = [("s_sc", exp_ev[(cur, 31)])] + bankB_ev[3]
                    for ktile in range(32):
                        ev_mm = S.op("tensor", lambda e, ktile=ktile:
                                     nc.tensor.matmul(
                                         row_ps, ones_c[:],
                                         pt_sb[:, ktile * 512:(ktile + 1) * 512],
                                         start=(ktile == 0), stop=(ktile == 31)),
                                     inc=("s_pe", 1) if ktile == 31 else None,
                                     waits=waits if ktile == 0 else ())
                        waits = ()
                    wsc = [("s_pe", ev_mm["s_pe"])]
                    if last_Aproj_ev[0]:
                        wsc.append(("s_pe", last_Aproj_ev[0]))
                    ev_l = S.op("scalar", lambda e: e.activation(
                        rowf[0:1, 0:512], row_ps, AF.Copy),
                        inc=("s_sc", 1), waits=wsc)
                    l_evac_ev[cur] = ev_l["s_sc"]
                    ev_lbf = S.op("scalar", lambda e: e.activation(
                        rowbf[0:1, 0:512], rowf[0:1, 0:512], AF.Copy),
                        inc=("s_sc", 1))
                    lbf_ev[cur] = ev_lbf["s_sc"]
                    bankB_ev[3] = [("s_sc", ev_l["s_sc"])]
                    lpar = g % 2
                    w = [("s_sc", ev_l["s_sc"])]
                    if l_read_ev[lpar]:
                        w.append(("s_ldr", l_read_ev[lpar]))
                    ev_lw = S.op("gpsimd", lambda e, lpar=lpar: e.dma_start(
                        out=d_l[lpar:lpar + 1, :], in_=rowf[0:1, 0:512]),
                        inc=("s_ldr", 16), waits=w)
                    w = [("s_ldr", ev_lw["s_ldr"])]
                    if recip_ev[0]:
                        w.append(("s_dv", recip_ev[0]))
                    ev_lr = S.op("gpsimd", lambda e, lpar=lpar: e.dma_start(
                        out=lt[:], in_=d_l[lpar].rearrange("(c p) -> p c", p=128),
                        allow_slow_non_contiguous=True),
                        inc=("s_ldr", 16), waits=w)
                    l_read_ev[lpar] = ev_lr["s_ldr"]
                    ev_rc = vop(lambda e: nc.vector.reciprocal(invl[:], lt[:]),
                                waits=[("s_ldr", ev_lr["s_ldr"])])
                    recip_ev[0] = ev_rc["s_dv"]
                    invl_of[cur] = ev_rc["s_dv"]

                    if nxt is not None:
                        emit_qnorm(nxt)


                if cur is not None:
                    started = [False] * 8
                    for j in range(8):
                        par = j % 2
                        for t4 in range(4):
                            ktile = j * 4 + t4
                            for c in range(8):
                                bk = bankA[c] if c < 4 else bankB[c - 4]
                                waits = []
                                if not started[c]:
                                    waits = [("s_kvr", v_load_ev[(cur, j)]),
                                             ("s_sc", exp_ev[(cur, j * 4 + 3)])]
                                    waits += (bankA_ev[c] if c < 4
                                              else bankB_ev[c - 4])
                                    started[c] = True
                                elif t4 == 0 and c == 0:
                                    waits = [("s_kvr", v_load_ev[(cur, j)]),
                                             ("s_sc", exp_ev[(cur, j * 4 + 3)])]
                                ev_mm = S.op("tensor", lambda e, j=j, t4=t4, c=c,
                                             par=par, bk=bk, ktile=ktile:
                                             nc.tensor.matmul(
                                                 bk,
                                                 v_b[:, par * 4096 + t4 * 1024 +
                                                     c * 128:
                                                     par * 4096 + t4 * 1024 +
                                                     (c + 1) * 128],
                                                 pt_sb[:, ktile * 512:
                                                       (ktile + 1) * 512],
                                                 start=(ktile == 0),
                                                 stop=(ktile == 31)),
                                             inc=("s_pe", 1) if
                                             (ktile == 31 or (t4 == 3 and c == 7))
                                             else None,
                                             waits=waits)
                        AVchunk_ev[(cur, j)] = ev_mm["s_pe"]
                        if j + 2 < 8:
                            emit_vload(cur, j + 2, AVchunk_ev[(cur, j)])
                    if nxt is not None:
                        emit_vload(nxt, 0, AVchunk_ev[(cur, 6)])
                        emit_vload(nxt, 1, AVchunk_ev[(cur, 7)])
                    av_last = AVchunk_ev[(cur, 7)]
                    w_extra = ([("s_pe", last_Aproj_ev[0])]
                               if last_Aproj_ev[0] else [])
                    ot_ready[cur] = [None] * 8
                    for c in range(4):
                        ev_o = vop(lambda e, c=c: nc.vector.tensor_copy(
                            ot_sb[:, c * 512:(c + 1) * 512], bankA[c]),
                            waits=[("s_pe", av_last)] + (w_extra if c == 0 else []))
                        ot_ready[cur][c] = ("s_dv", ev_o["s_dv"])
                        bankA_ev[c] = [("s_dv", ev_o["s_dv"])]
                    for c in range(4, 8):
                        ev_o = S.op("scalar", lambda e, c=c: e.activation(
                            ot_sb[:, c * 512:(c + 1) * 512], bankB[c - 4],
                            AF.Copy), inc=("s_sc", 1),
                            waits=[("s_pe", av_last)] +
                                  (w_extra if c == 4 else []))
                        ot_ready[cur][c] = ("s_sc", ev_o["s_sc"])
                        bankB_ev[c - 4] = [("s_sc", ev_o["s_sc"])]

                    apj_banks = [(bankB, 0, 1), (bankB, 2, 3),
                                 (bankA, 0, 1), (bankA, 2, 3)]

                    def emit_aproj(s_i):
                        bks, i0, i1 = apj_banks[s_i]
                        bev = bankB_ev if bks is bankB else bankA_ev
                        ev_mm = None
                        for half, bi in ((0, i0), (1, i1)):
                            bkp = bks[bi]
                            first_w = [("s_sc", lbf_ev[cur]),
                                       ("s_dv", bias_bf_ev)] + bev[bi]
                            for c in range(8):
                                w = list(first_w) + [ot_ready[cur][c]]
                                first_w = []
                                S.op("tensor", lambda e, c=c, s_i=s_i, bkp=bkp,
                                     half=half: nc.tensor.matmul(
                                         bkp,
                                         ot_sb[:, c * 512 + s_i * 128:
                                               c * 512 + (s_i + 1) * 128],
                                         wo_sb[:, c * 1024 + half * 512:
                                               c * 1024 + (half + 1) * 512],
                                         start=(c == 0), stop=False),
                                     waits=w)
                            ev_mm = S.op("tensor", lambda e, s_i=s_i, bkp=bkp,
                                         half=half: nc.tensor.matmul(
                                             bkp,
                                             rowbf[0:1, s_i * 128:
                                                   (s_i + 1) * 128],
                                             borow[0:1, half * 512:
                                                   (half + 1) * 512],
                                             start=False, stop=True),
                                         inc=("s_pe", 1))
                        aproj_stop[(cur, s_i)] = ev_mm["s_pe"]
                        last_Aproj_ev[0] = ev_mm["s_pe"]

                    emit_xload(cur, 0)
                    emit_xload(cur, 1)
                    emit_aproj(0)
                    emit_aproj(1)
                    emit_aproj(2)
                    emit_aproj(3)


            S.wait("gpsimd", "s_ow", S.cnt["s_ow"])

            with nc.Block() as blockB:
                S.flush(blockB)

    return nc


def kernel(**inputs):
    x = np.asarray(inputs["L0_attn_x"])
    wq = np.asarray(inputs["L0_attn_Wq"])
    wk = np.asarray(inputs["L0_attn_Wk"])
    wv = np.asarray(inputs["L0_attn_Wv"])
    wo = np.asarray(inputs["L0_attn_Wo"])
    bq = np.asarray(inputs["L0_attn_bq"])
    bk = np.asarray(inputs["L0_attn_bk"])
    bv = np.asarray(inputs["L0_attn_bv"])
    bo = np.asarray(inputs["L0_attn_bo"])
    wf = np.asarray(inputs["L0_ffn_weight"])
    bf = np.asarray(inputs["L0_ffn_bias"])

    wqT = np.ascontiguousarray(wq.T)
    wkT = np.ascontiguousarray(wk.T)
    wvT = np.ascontiguousarray(wv.T)
    woT = np.ascontiguousarray(wo.T)
    bql = np.ascontiguousarray(bq.reshape(8, 128).T)
    bkl = np.ascontiguousarray(bk.reshape(8, 128).T)
    bvr = np.ascontiguousarray(bv.reshape(1, H))
    bor = np.ascontiguousarray(bo.reshape(1, H))
    bfr = np.ascontiguousarray(bf.reshape(1, H))

    nc = build_nc()
    in_maps = []
    for i in range(NCORES):
        xs = np.ascontiguousarray(x[BPC * i:BPC * (i + 1)].reshape(BPC * SEQ, H))
        in_maps.append({
            "x": xs, "wqT": wqT, "wkT": wkT, "wvT": wvT, "woT": woT,
            "wf": np.ascontiguousarray(wf),
            "bql": bql, "bkl": bkl, "bvr": bvr, "bor": bor, "bfr": bfr,
        })
    res = run_bass_kernel_spmd(nc, in_maps, core_ids=list(range(NCORES)))
    out = np.stack([np.asarray(r["out"]) for r in res.results])
    return out.reshape(16, SEQ, H).astype(np.float32, copy=False)
